# version 09
# baseline (speedup 1.0000x reference)
from contextlib import ExitStack

import numpy as np
import ml_dtypes

import concourse.bass as bass
import concourse.mybir as mybir
from concourse.bass_utils import run_bass_kernel_spmd

F32 = mybir.dt.float32
BF16 = mybir.dt.bfloat16
U8 = mybir.dt.uint8
AF = mybir.ActivationFunctionType
ALU = mybir.AluOpType
AX = mybir.AxisListType

D = 4096
S = 2048
NH = 16
HD = 256
KD = D // 128
MYT = 1024
EPS = 1e-5
NEGV = -1e9
SCL = 1.0 / 16.0

BFNP = np.dtype(ml_dtypes.bfloat16)


def _split512(n):
    out = []
    while n > 0:
        c = min(n, 512)
        out.append(c)
        n -= c
    return out


class Sched:

    def __init__(self):
        self.mode = "count"
        self.cnt = {}
        self.ms = {}
        self.eng = None
        self.engname = None
        self.sems = None
        self.waited = None

    def begin_count(self):
        self.mode = "count"
        self.cnt = {}
        self.ms = {}

    def begin_emit(self, engname, eng, sems):
        self.mode = "emit"
        self.engname = engname
        self.eng = eng
        self.sems = sems
        self.waited = {}

    def op(self, engname, fn, waits=(), inc=None, ms=None):
        if self.mode == "count":
            for k in waits:
                assert k in self.ms, f"wait on undefined milestone {k}"
            if inc is not None:
                s, n = inc
                self.cnt[s] = self.cnt.get(s, 0) + n
                if ms is not None:
                    for k in (ms if isinstance(ms, list) else [ms]):
                        assert k not in self.ms, f"dup milestone {k}"
                        self.ms[k] = (s, self.cnt[s])
            else:
                assert ms is None, ms
            return
        if engname != self.engname:
            return
        for k in waits:
            s, v = self.ms[k]
            if v > self.waited.get(s, 0):
                self.eng.wait_ge(self.sems[s], v)
                self.waited[s] = v
        instr = fn(self.eng)
        if inc is not None:
            s, n = inc
            instr.then_inc(self.sems[s], n)

    def final_wait(self, engname, semname):
        if self.mode == "count" or engname != self.engname:
            return
        v = self.cnt.get(semname, 0)
        if v > self.waited.get(semname, 0):
            self.eng.wait_ge(self.sems[semname], v)
            self.waited[semname] = v


def build_nc(taps=()):
    nc = bass.Bass("TRN2", target_bir_lowering=False)

    xT_d = nc.dram_tensor("xT", [D, S], F32, kind="ExternalInput").ap()
    xres_d = nc.dram_tensor("xres", [MYT, D], F32, kind="ExternalInput").ap()
    wq_d = nc.dram_tensor("wq", [D, D], F32, kind="ExternalInput").ap()
    wk_d = nc.dram_tensor("wk", [D, D], F32, kind="ExternalInput").ap()
    wv_d = nc.dram_tensor("wv", [D, D], F32, kind="ExternalInput").ap()
    wo_d = nc.dram_tensor("wo", [D, D], F32, kind="ExternalInput").ap()
    fw_d = nc.dram_tensor("fw", [D, D], U8, kind="ExternalInput").ap()
    fs_d = nc.dram_tensor("fs", [1, 1], F32, kind="ExternalInput").ap()
    tabq_d = nc.dram_tensor("tabq", [128, MYT], BF16, kind="ExternalInput").ap()
    tabk_d = nc.dram_tensor("tabk", [128, S], BF16, kind="ExternalInput").ap()
    maska_d = nc.dram_tensor("maska", [128, 128], BF16, kind="ExternalInput").ap()
    maskb_d = nc.dram_tensor("maskb", [128, 128], BF16, kind="ExternalInput").ap()
    ident_d = nc.dram_tensor("ident", [128, 128], BF16, kind="ExternalInput").ap()
    out_d = nc.dram_tensor("out", [MYT, D], F32, kind="ExternalOutput").ap()

    qspill = nc.dram_tensor("qspill", [NH, HD, MYT], BF16, kind="Internal").ap()
    kspill = nc.dram_tensor("kspill", [NH, HD, S], BF16, kind="Internal").ap()
    vspill = nc.dram_tensor("vspill", [NH, 128, 16 * HD], BF16, kind="Internal").ap()
    z_hbm = nc.dram_tensor("z_hbm", [8, 128, D], F32, kind="Internal").ap()
    x2_hbm = nc.dram_tensor("x2_hbm", [8, 128, D], F32, kind="Internal").ap()
    z2_hbm = nc.dram_tensor("z2_hbm", [8, 128, D], F32, kind="Internal").ap()

    tap_outs = {}
    if "qkv" in taps:
        tap_outs["tap_q"] = nc.dram_tensor("tap_q", [NH, HD, MYT], BF16, kind="ExternalOutput").ap()
        tap_outs["tap_k"] = nc.dram_tensor("tap_k", [NH, HD, S], BF16, kind="ExternalOutput").ap()
        tap_outs["tap_v"] = nc.dram_tensor("tap_v", [NH, 128, 16 * HD], BF16, kind="ExternalOutput").ap()
    if "cat" in taps:
        tap_outs["tap_cat"] = nc.dram_tensor("tap_cat", [128, 32768], BF16, kind="ExternalOutput").ap()
    if "z" in taps:
        tap_outs["tap_z"] = nc.dram_tensor("tap_z", [8, 128, D], F32, kind="ExternalOutput").ap()
    if "x2" in taps:
        tap_outs["tap_x2"] = nc.dram_tensor("tap_x2", [8, 128, D], F32, kind="ExternalOutput").ap()
    if "z2" in taps:
        tap_outs["tap_z2"] = nc.dram_tensor("tap_z2", [8, 128, D], F32, kind="ExternalOutput").ap()

    es = ExitStack()
    A = es.enter_context(nc.sbuf_tensor("arenaA", [128, 65536], BF16))
    Bu = es.enter_context(nc.sbuf_tensor("arenaB", [128, 65536], U8))
    Bf = Bu.bitcast(F32)
    Bb = Bu.bitcast(BF16)
    identb = es.enter_context(nc.sbuf_tensor("identb", [128, 128], BF16))
    maska = es.enter_context(nc.sbuf_tensor("maska_s", [128, 128], BF16))
    maskb = es.enter_context(nc.sbuf_tensor("maskb_s", [128, 128], BF16))
    tabqc = es.enter_context(nc.sbuf_tensor("tabqc", [64, MYT], BF16))
    tabqs = es.enter_context(nc.sbuf_tensor("tabqs", [64, MYT], BF16))
    tabkc = es.enter_context(nc.sbuf_tensor("tabkc", [64, S], BF16))
    tabks = es.enter_context(nc.sbuf_tensor("tabks", [64, S], BF16))
    stats = es.enter_context(nc.sbuf_tensor("stats", [128, 512], F32))
    ones1 = es.enter_context(nc.sbuf_tensor("ones1", [1, 128], F32))

    ST_BN = 0
    ST_MV = 384
    ST_RSTD = 400
    ST_NMR = 408
    ST_RS = 416
    ST_NB = 424
    ST_FS = 432
    ST_FSB = 433
    ST_FS2 = 434
    ST_NEGFS = 435
    ST_EPS = 436
    ST_RSP = 440

    PL = es.enter_context(nc.psum_tensor("PL", [128, 2048], F32))
    pb = [es.enter_context(nc.psum_tensor(f"pb{i}", [128, 512], F32)) for i in range(4)]
    trP = [p.bitcast(BF16) for p in pb[:2]]
    avP = [pb[2], pb[3]]

    def bank(i):
        if i < 4:
            return PL[:, i * 512:(i + 1) * 512]
        return pb[i - 4][:, :]

    CAT0 = 0
    QKVB = 32768
    SCR = QKVB + 2 * 10240
    AU0 = SCR
    AN0 = SCR + 2048
    AT0 = SCR + 4096
    X2T0 = 32768

    XST, WQST, WKST, WVST = 0, 1024, 2048, 2560
    WQBF, WKBF, WVBF = 7168, 8192, 8704
    QSTG, KSTG, VSTG = 9728, 13824, 22016
    RPQ, RPK, SCQ, SCK = 0, 1024, 3072, 5120
    SV = 4608
    WOST, ZST = 0, 1024
    WOBF = 6144
    ZSL, XSL, TSL, X2F = 0, 2048, 4096, 5120
    X2BF = 12544
    FWST = 0
    FWBF = 512
    Z2ST = 5120
    OST = 5120

    E = Sched()

    def walk():
        QK = {"sync": 12, "gpsimd": 8}
        qc = {"sync": 0, "gpsimd": 0}

        def dma(q, out, in_, waits=(), ms=None):
            j = qc[q]
            qc[q] = j + 1
            sem = f"{q[0]}{j % QK[q]}"
            w = list(waits)
            if j >= QK[q]:
                w.append(("_dma", q, j - QK[q]))
            mss = [("_dma", q, j)] + ([ms] if ms is not None else [])
            E.op(q, lambda e, o=out, i=in_: e.dma_start(out=o, in_=i),
                 w, inc=(sem, 16), ms=mss)

        def op(eng, fn, waits=(), ms=None):
            semname = {"tensor": "t", "vector": "v", "scalar": "s", "gpsimd": "p"}[eng]
            E.op(eng, fn, waits, inc=(semname, 1) if ms is not None else None, ms=ms)

        dma("sync", identb[:, :], ident_d, ms=("d_ident",))
        dma("sync", maska[:, :], maska_d, ms=("d_maska",))
        dma("sync", maskb[:, :], maskb_d, ms=("d_maskb",))
        dma("sync", tabqc[:, :], tabq_d[0:64, :], ms=("tabq",))
        dma("sync", tabqs[:, :], tabq_d[64:128, :], ms=("tabqs",))
        dma("sync", tabkc[:, :], tabk_d[0:64, :], ms=("tabk",))
        dma("sync", tabks[:, :], tabk_d[64:128, :], ms=("tabks",))
        dma("sync", stats[0:1, ST_FS:ST_FS + 1], fs_d, ms=("d_fs",))
        op("vector", lambda e: e.memset(ones1[:, :], 1.0), ms=("ones",))
        op("vector", lambda e: e.memset(stats[:, ST_EPS:ST_EPS + 1], float(EPS)), ms=("epsv",))
        op("tensor", lambda e: e.matmul(pb[0][:, 0:1], ones1[0:1, :],
                                        stats[0:1, ST_FS:ST_FS + 1], start=True, stop=True),
           waits=[("ones",), ("d_fs",)], ms=("fsmm",))
        op("vector", lambda e: e.tensor_copy(stats[:, ST_FSB:ST_FSB + 1], pb[0][:, 0:1]),
           waits=[("fsmm",)], ms=("fsb",))
        op("vector", lambda e: e.tensor_scalar_mul(stats[:, ST_FS2:ST_FS2 + 1],
                                                   stats[:, ST_FSB:ST_FSB + 1], 2.0),
           waits=[("fsb",)], ms=("fs2",))
        op("vector", lambda e: e.tensor_scalar_mul(stats[:, ST_NEGFS:ST_NEGFS + 1],
                                                   stats[:, ST_FSB:ST_FSB + 1], -1.0),
           waits=[("fsb",)], ms=("negfs",))

        XST = 0
        WQST = 2048
        WKST = 4096
        WVST = 5120
        WQBF = 14336
        WKBF = 16384
        WVBF = 17408
        QSTG = 19456
        KSTG = 23552
        VSTG = 27648

        for ii in range(64):
            kk, hf = ii // 2, ii % 2
            b = ii % 2
            w = [("xcast", ii - 2)] if ii >= 2 else []
            dma("gpsimd", Bf[:, XST + b * 1024:XST + (b + 1) * 1024],
                xT_d[kk * 128:(kk + 1) * 128, hf * 1024:(hf + 1) * 1024],
                waits=w, ms=("xdma", ii))
            op("scalar",
               lambda e, kk=kk, hf=hf, b=b: e.copy(
                   A[:, kk * 2048 + hf * 1024:kk * 2048 + (hf + 1) * 1024],
                   Bf[:, XST + b * 1024:XST + (b + 1) * 1024]),
               waits=[("xdma", ii)], ms=("xcast", ii))

        for p in range(8):
            h0 = 2 * p
            for k2 in range(16):
                b = k2 % 2
                if k2 >= 2:
                    w = [("qcast", p, k2 - 2)]
                elif p > 0:
                    w = [("qmm", p - 1, 15)]
                else:
                    w = []
                dma("sync", Bf[:, WQST + b * 1024:WQST + (b + 1) * 1024].rearrange("p (c m) -> p c m", m=512),
                    wq_d[k2 * 256:(k2 + 1) * 256, h0 * HD:h0 * HD + 512].rearrange("(c p) m -> p c m", p=128),
                    waits=w, ms=("qwd", p, k2))
                wc = [("qwd", p, k2)]
                if k2 >= 2:
                    wc.append(("qmm", p, k2 - 2))
                op("vector",
                   lambda e, b=b: e.tensor_copy(Bb[:, WQBF + b * 1024:WQBF + (b + 1) * 1024],
                                                Bf[:, WQST + b * 1024:WQST + (b + 1) * 1024]),
                   waits=wc, ms=("qcast", p, k2))
                for sub in range(2):
                    kk = 2 * k2 + sub
                    for m in range(4):
                        for n in range(2):
                            wmm = [("qcast", p, k2), ("xcast", kk * 2), ("xcast", kk * 2 + 1)]
                            if kk == 0:
                                wmm.append(("vevb", p - 1, 1, m * 2 + n) if p > 0 else ("fsb",))
                            op("tensor",
                               lambda e, b=b, sub=sub, m=m, n=n, kk=kk: e.matmul(
                                   bank(m * 2 + n),
                                   Bb[:, WQBF + b * 1024 + sub * 512 + m * 128:
                                      WQBF + b * 1024 + sub * 512 + (m + 1) * 128],
                                   A[:, kk * 2048 + n * 512:kk * 2048 + (n + 1) * 512],
                                   start=(kk == 0), stop=(kk == KD - 1)),
                               waits=wmm if (kk == 0 or (m == 0 and n == 0)) else (),
                               ms=("qmm", p, k2) if (m == 3 and n == 1 and sub == 1) else None)
            for m in range(4):
                for n in range(2):
                    w = [("qmm", p, 15)]
                    if p > 0:
                        w += [("qspill", p - 1, 0), ("qspill", p - 1, 1)]
                    op("vector",
                       lambda e, m=m, n=n: e.tensor_scalar_mul(
                           Bb[:, QSTG + m * 1024 + n * 512:QSTG + m * 1024 + (n + 1) * 512],
                           bank(m * 2 + n), SCL),
                       waits=w, ms=("qevb", p, m * 2 + n))
            for hh in range(2):
                dma("gpsimd",
                    qspill[h0 + hh].rearrange("(c p) t -> p c t", p=128),
                    Bb[:, QSTG + hh * 2048:QSTG + (hh + 1) * 2048].rearrange("p (c t) -> p c t", t=1024),
                    waits=[("qevb", p, i) for i in range(8)],
                    ms=("qspill", p, hh))

            for hh in range(2):
                h = h0 + hh
                for k2 in range(16):
                    b = k2 % 2
                    if k2 >= 2:
                        w = [("kcast", p, hh, k2 - 2)]
                    elif hh == 1:
                        w = [("kmm", p, 0, 14 + k2)]
                    elif p > 0:
                        w = [("kmm", p - 1, 1, 14 + k2)]
                    else:
                        w = []
                    dma("sync", Bf[:, WKST + b * 512:WKST + (b + 1) * 512].rearrange("p (c m) -> p c m", m=256),
                        wk_d[k2 * 256:(k2 + 1) * 256, h * HD:(h + 1) * HD].rearrange("(c p) m -> p c m", p=128),
                        waits=w, ms=("kwd", p, hh, k2))
                    wc = [("kwd", p, hh, k2)]
                    if k2 >= 2:
                        wc.append(("kmm", p, hh, k2 - 2))
                    op("vector",
                       lambda e, b=b: e.tensor_copy(Bb[:, WKBF + b * 512:WKBF + (b + 1) * 512],
                                                    Bf[:, WKST + b * 512:WKST + (b + 1) * 512]),
                       waits=wc, ms=("kcast", p, hh, k2))
                    for sub in range(2):
                        kk = 2 * k2 + sub
                        for m in range(2):
                            for n in range(4):
                                wmm = [("kcast", p, hh, k2), ("xcast", kk * 2), ("xcast", kk * 2 + 1)]
                                if kk == 0:
                                    wmm.append(("qevb", p, m * 4 + n) if hh == 0
                                               else ("kevb", p, 0, m * 4 + n))
                                op("tensor",
                                   lambda e, b=b, sub=sub, m=m, n=n, kk=kk: e.matmul(
                                       bank(m * 4 + n),
                                       Bb[:, WKBF + b * 512 + sub * 256 + m * 128:
                                          WKBF + b * 512 + sub * 256 + (m + 1) * 128],
                                       A[:, kk * 2048 + n * 512:kk * 2048 + (n + 1) * 512],
                                       start=(kk == 0), stop=(kk == KD - 1)),
                                   waits=wmm if (kk == 0 or (m == 0 and n == 0)) else (),
                                   ms=("kmm", p, hh, k2) if (m == 1 and n == 3 and sub == 1) else None)
                for m in range(2):
                    for n in range(4):
                        w = [("kmm", p, hh, 15)]
                        if hh == 1:
                            w.append(("kspill", p, 0))
                        elif p > 0:
                            w.append(("kspill", p - 1, 1))
                        op("vector",
                           lambda e, m=m, n=n: e.tensor_copy(
                               Bb[:, KSTG + m * 2048 + n * 512:KSTG + m * 2048 + (n + 1) * 512],
                               bank(m * 4 + n)),
                           waits=w, ms=("kevb", p, hh, m * 4 + n))
                dma("gpsimd",
                    kspill[h].rearrange("(c p) t -> p c t", p=128),
                    Bb[:, KSTG:KSTG + 4096].rearrange("p (c t) -> p c t", t=2048),
                    waits=[("kevb", p, hh, i) for i in range(8)],
                    ms=("kspill", p, hh))

            for ps in range(2):
                for k2 in range(16):
                    b = k2 % 2
                    if k2 >= 2:
                        w = [("vcast", p, ps, k2 - 2)]
                    elif ps == 1:
                        w = [("vmm", p, 0, 14 + k2)]
                    elif p > 0:
                        w = [("vmm", p - 1, 1, 14 + k2)]
                    else:
                        w = []
                    dma("sync", Bf[:, WVST + b * 1024:WVST + (b + 1) * 1024].rearrange("p (c m) -> p c m", m=512),
                        wv_d[k2 * 256:(k2 + 1) * 256, h0 * HD:h0 * HD + 512].rearrange("(c p) m -> p c m", p=128),
                        waits=w, ms=("vwd", p, ps, k2))
                    wc = [("vwd", p, ps, k2)]
                    if k2 >= 2:
                        wc.append(("vmm", p, ps, k2 - 2))
                    op("vector",
                       lambda e, b=b: e.tensor_copy(Bb[:, WVBF + b * 1024:WVBF + (b + 1) * 1024],
                                                    Bf[:, WVST + b * 1024:WVST + (b + 1) * 1024]),
                       waits=wc, ms=("vcast", p, ps, k2))
                    for sub in range(2):
                        kk = 2 * k2 + sub
                        for bidx in range(8):
                            c = ps * 8 + bidx
                            wmm = [("vcast", p, ps, k2), ("xcast", kk * 2), ("xcast", kk * 2 + 1)]
                            if kk == 0:
                                wmm.append(("kevb", p, 1, bidx) if ps == 0 else ("vevb", p, 0, bidx))
                            op("tensor",
                               lambda e, b=b, sub=sub, c=c, bidx=bidx, kk=kk: e.matmul(
                                   bank(bidx),
                                   A[:, kk * 2048 + c * 128:kk * 2048 + (c + 1) * 128],
                                   Bb[:, WVBF + b * 1024 + sub * 512:WVBF + b * 1024 + (sub + 1) * 512],
                                   start=(kk == 0), stop=(kk == KD - 1)),
                               waits=wmm if (kk == 0 or bidx == 0) else (),
                               ms=("vmm", p, ps, k2) if (bidx == 7 and sub == 1) else None)
                for bidx in range(8):
                    w = [("vmm", p, ps, 15)]
                    if ps == 1:
                        w += [("vspill", p, 0, 0), ("vspill", p, 0, 1)]
                    elif p > 0:
                        w += [("vspill", p - 1, 1, 0), ("vspill", p - 1, 1, 1)]
                    op("vector",
                       lambda e, bidx=bidx: e.tensor_copy(
                           Bb[:, VSTG + bidx * 512:VSTG + (bidx + 1) * 512], bank(bidx)),
                       waits=w, ms=("vevb", p, ps, bidx))
                for hh in range(2):
                    h = h0 + hh
                    dma("gpsimd",
                        vspill[h].rearrange("p (c d) -> p c d", d=HD)[:, ps * 8:(ps + 1) * 8, :],
                        Bb[:, VSTG:VSTG + 8 * 512].rearrange("p (c two) -> p c two", two=512)[:, :, hh * HD:(hh + 1) * HD],
                        waits=[("vevb", p, ps, i) for i in range(8)],
                        ms=("vspill", p, ps, hh))

        p1done = ("vspill", 7, 1, 1)
        p1all = [("vspill", 7, 1, 0), ("vspill", 7, 1, 1)]

        grp_hist = []

        def qkvset(h):
            si = h % 3
            if si < 2:
                return A, QKVB + si * 10240
            return Bb, 9344

        def emit_head_loads(h):
            AR, qb = qkvset(h)
            kb = qb + 2048
            vb = qb + 6144
            wl = [("qspill", h // 2, h % 2), ("kspill", h // 2, h % 2),
                  ("vspill", h // 2, 0, h % 2), ("vspill", h // 2, 1, h % 2)] + p1all
            if h >= 3:
                wl.append(("avmm", (h - 3) * 8 + 7 + 1 - 1))
            dma("gpsimd", AR[:, qb:qb + 2048].rearrange("p (c t) -> p c t", t=1024),
                qspill[h].rearrange("(c p) t -> p c t", p=128),
                waits=wl, ms=("ldq", h))
            dma("gpsimd", AR[:, kb:kb + 4096].rearrange("p (c t) -> p c t", t=2048),
                kspill[h].rearrange("(c p) t -> p c t", p=128),
                waits=wl, ms=("ldk", h))
            dma("gpsimd", AR[:, vb:vb + 4096], vspill[h],
                waits=wl, ms=("ldv", h))
            wr = [("ldq", h)] + ([("ropeq", h - 1)] if h >= 1 else [])
            dma("gpsimd", Bb[0:32, 3072:4096], AR[32:64, qb:qb + 1024], waits=wr, ms=("swq1", h))
            dma("gpsimd", Bb[32:64, 3072:4096], AR[0:32, qb:qb + 1024], waits=wr, ms=("swq2", h))
            wr = [("ldk", h)] + ([("ropek", h - 1)] if h >= 1 else [])
            dma("gpsimd", Bb[0:32, 5120:7168], AR[32:64, kb:kb + 2048], waits=wr, ms=("swk1", h))
            dma("gpsimd", Bb[32:64, 5120:7168], AR[0:32, kb:kb + 2048], waits=wr, ms=("swk2", h))
            wv_ = [("swq1", h), ("swq2", h), ("tabq",), ("tabqs",)] + ([("ropeq", h - 1)] if h >= 1 else [])
            op("gpsimd", lambda e, AR=AR, qb=qb: e.tensor_mul(Bb[0:64, 4096:5120],
                                                              AR[0:64, qb:qb + 1024], tabqc[:, :]),
               waits=wv_, ms=("rq1", h))
            op("gpsimd", lambda e: e.tensor_mul(Bb[0:64, 3072:4096],
                                                Bb[0:64, 3072:4096], tabqs[:, :]),
               waits=[("rq1", h)] + wv_, ms=("rq2", h))
            op("gpsimd", lambda e, AR=AR, qb=qb: e.tensor_add(AR[0:64, qb:qb + 1024],
                                                              Bb[0:64, 4096:5120],
                                                              Bb[0:64, 3072:4096]),
               waits=[("rq1", h), ("rq2", h)], ms=("ropeq", h))
            wv_ = [("swk1", h), ("swk2", h), ("tabk",), ("tabks",)] + ([("ropek", h - 1)] if h >= 1 else [])
            op("gpsimd", lambda e, AR=AR, kb=kb: e.tensor_mul(Bb[0:64, 7168:9216],
                                                              AR[0:64, kb:kb + 2048], tabkc[:, :]),
               waits=wv_, ms=("rk1", h))
            op("gpsimd", lambda e: e.tensor_mul(Bb[0:64, 5120:7168],
                                                Bb[0:64, 5120:7168], tabks[:, :]),
               waits=[("rk1", h)] + wv_, ms=("rk2", h))
            op("gpsimd", lambda e, AR=AR, kb=kb: e.tensor_add(AR[0:64, kb:kb + 2048],
                                                              Bb[0:64, 7168:9216],
                                                              Bb[0:64, 5120:7168]),
               waits=[("rk1", h), ("rk2", h)], ms=("ropek", h))

        def emit_lg(t, P):
            h, jj = divmod(t, 8)
            AR, qb = qkvset(h)
            kb = qb + 2048
            klen = jj + 1
            KW = klen * 128
            wlog = [("ropeq", h), ("ropek", h), ("d_maska",), ("d_maskb",), ("d_ident",)]
            wlog.append(("exp", t - 1) if t > 0 else p1done)
            first = True
            pbase = P * 1024
            koff = P * 1024
            coff = 0
            chunks = _split512(KW)
            for ci, csz in enumerate(chunks):
                isdiag = (ci == len(chunks) - 1)
                for m in range(2):
                    stop = (m == 1 and not isdiag)
                    op("tensor",
                       lambda e, m=m, coff=coff, csz=csz, pbase=pbase, koff=koff,
                       qb=qb, kb=kb, jj=jj, stop=stop, AR=AR: e.matmul(
                           PL[:, pbase + coff:pbase + coff + csz],
                           AR[:, qb + m * 1024 + jj * 128:qb + m * 1024 + (jj + 1) * 128],
                           AR[:, kb + m * 2048 + koff + coff:kb + m * 2048 + koff + coff + csz],
                           start=(m == 0), stop=stop),
                       waits=wlog if first else ())
                    first = False
                if isdiag:
                    mk = maska if P == 0 else maskb
                    mend = pbase + KW
                    op("tensor",
                       lambda e, mk=mk, mend=mend: e.matmul(
                           PL[:, mend - 128:mend], identb[:, :], mk[:, :],
                           start=False, stop=True),
                       ms=("lgp", t, P))
                coff += csz

        def emit_mx(t):
            h, jj = divmod(t, 8)
            gp = t % 3
            KW = (jj + 1) * 128
            SVX = 4608 + 16 * gp
            wmx = [("lgp", t, 0), ("lgp", t, 1)]
            if t >= 3:
                wmx.append(("exp", t - 3))
            op("vector", lambda e, KW=KW, SVX=SVX: e.reduce_max(
                   Bf[:, SVX:SVX + 1],
                   PL.rearrange("p (c k) -> p c k", k=1024)[:, :, 0:KW],
                   axis=AX.XY, negate=True),
               waits=wmx, ms=("mx", t))

        def emit_smx(t):
            h, jj = divmod(t, 8)
            gp = t % 3
            KW = (jj + 1) * 128
            W = 2 * KW
            SVX = 4608 + 16 * gp
            AUB = 21632 + 2048 * gp
            AN = SCR + 2048 * gp
            wexp = [("mx", t)]
            if t >= 3:
                wexp += [("nrm", t - 3), ("rcp", t - 3)]
            op("scalar", lambda e, KW=KW, SVX=SVX, AUB=AUB: e.activation(
                   Bb[:, AUB:AUB + 2 * KW].rearrange("p (c k) -> p c k", k=KW),
                   PL.rearrange("p (c k) -> p c k", k=1024)[:, :, 0:KW],
                   AF.Exp, bias=Bf[:, SVX:SVX + 1], scale=1.0,
                   accum_out=Bf[:, SVX + 5:SVX + 6]),
               waits=wexp, ms=("exp", t))
            op("vector", lambda e, SVX=SVX: e.reciprocal(
                   Bf[:, SVX + 6:SVX + 7], Bf[:, SVX + 5:SVX + 6]),
               waits=[("exp", t)], ms=("rcp", t))
            wn = [("rcp", t)]
            if t >= 3:
                wn.append(("trl", *divmod(t - 3, 8)))
            op("vector", lambda e, W=W, AUB=AUB, AN=AN, SVX=SVX: e.tensor_scalar_mul(
                   A[:, AN:AN + W], Bb[:, AUB:AUB + W], Bf[:, SVX + 6:SVX + 7]),
               waits=wn, ms=("nrm", t))

        def emit_trav(t):
            h, jj = divmod(t, 8)
            AR, qb0 = qkvset(h)
            vb = qb0 + 6144
            gp = t % 3
            klen = jj + 1
            KK = 2 * klen
            AN = SCR + 2048 * gp
            AT = SCR + 6144 + 2048 * gp
            ngrp = (KK + 3) // 4
            for g in range(ngrp):
                kks = list(range(4 * g, min(4 * g + 4, KK)))
                tb = len(grp_hist) % 2
                wtr = [("nrm", t)]
                if len(grp_hist) >= 2:
                    wtr.append(grp_hist[-2])
                for i, kk in enumerate(kks):
                    last_of_task = (kk == KK - 1)
                    op("tensor",
                       lambda e, tb=tb, kk=kk, i=i, AN=AN: e.transpose(
                           trP[tb][:, i * 128:(i + 1) * 128],
                           A[:, AN + kk * 128:AN + (kk + 1) * 128], identb[:, :]),
                       waits=wtr if i == 0 else (),
                       ms=("trl", h, jj) if last_of_task else ("trx", h, jj, kk))
                wge = [("trl", h, jj) if kks[-1] == KK - 1 else ("trx", h, jj, kks[-1])]
                if t >= 3:
                    wge.append(("avmm", t - 3))
                gkey = ("grpev", t, g)
                op("vector",
                   lambda e, tb=tb, g=g, n=len(kks), AT=AT: e.tensor_copy(
                       A[:, AT + g * 512:AT + g * 512 + n * 128],
                       trP[tb][:, 0:n * 128]),
                   waits=wge, ms=gkey)
                grp_hist.append(gkey)
            for kk in range(KK):
                c = kk if kk < klen else 8 + (kk - klen)
                wav = [("grpev", t, kk // 4)]
                if kk == 0:
                    wav.append(("avev", t - 1) if t > 0 else p1done)
                    wav.append(("ldv", h))
                for dh in range(2):
                    op("tensor",
                       lambda e, dh=dh, c=c, kk=kk, KK=KK, vb=vb, AT=AT, AR=AR: e.matmul(
                           avP[dh][:, 0:128],
                           AR[:, vb + c * 256 + dh * 128:vb + c * 256 + (dh + 1) * 128],
                           A[:, AT + kk * 128:AT + (kk + 1) * 128],
                           start=(kk == 0), stop=(kk == KK - 1)),
                       waits=wav if (dh == 0 or kk == 0) else (),
                       ms=("avmm", t) if (kk == KK - 1 and dh == 1) else None)
        def emit_avev(t):
            h, jj = divmod(t, 8)
            for dh in range(2):
                op("scalar",
                   lambda e, dh=dh, h=h, jj=jj: e.copy(
                       A[:, CAT0 + (2 * h + dh) * 1024 + jj * 128:
                          CAT0 + (2 * h + dh) * 1024 + (jj + 1) * 128],
                       avP[dh][:, 0:128]),
                   waits=[("avmm", t)], ms=("avev", t) if dh == 1 else None)

        emit_head_loads(0)
        emit_head_loads(1)
        emit_head_loads(2)
        T2 = NH * 8
        for u in range(T2 + 2):
            if u < T2:
                h, jj = divmod(u, 8)
                if jj == 3 and 3 <= h + 2 < NH:
                    emit_head_loads(h + 2)
                emit_lg(u, 0)
                emit_lg(u, 1)
                emit_mx(u)
            if u - 2 >= 0:
                emit_trav(u - 2)
            if u < T2:
                emit_smx(u)
            if u - 2 >= 0:
                emit_avev(u - 2)

        p2done = ("avev", NH * 8 - 1)

        WOST4 = 0
        WOBF4 = 4096
        ZST4 = 3584
        XRS4 = 5632
        for nn in range(8):
            for k2 in range(16):
                b = k2 % 2
                if k2 >= 2:
                    w = [("wocast", nn, k2 - 2)]
                elif nn > 0:
                    w = [("wocast", nn - 1, 14 + k2)]
                else:
                    w = [p2done]
                dma("sync", Bf[:, WOST4 + b * 1024:WOST4 + (b + 1) * 1024].rearrange("p (c m) -> p c m", m=512),
                    wo_d[k2 * 256:(k2 + 1) * 256, nn * 512:(nn + 1) * 512].rearrange("(c p) m -> p c m", p=128),
                    waits=w, ms=("wod", nn, k2))
                wc = [("wod", nn, k2)]
                if k2 >= 2:
                    wc.append(("womm", nn, k2 - 2))
                op("scalar",
                   lambda e, b=b: e.copy(Bb[:, WOBF4 + b * 1024:WOBF4 + (b + 1) * 1024],
                                         Bf[:, WOST4 + b * 1024:WOST4 + (b + 1) * 1024]),
                   waits=wc, ms=("wocast", nn, k2))
                for sub in range(2):
                    kk = 2 * k2 + sub
                    for m in range(8):
                        wmm = [("wocast", nn, k2), p2done]
                        if kk == 0 and nn > 0:
                            wmm.append(("zev", nn - 1, m))
                        op("tensor",
                           lambda e, m=m, b=b, sub=sub, kk=kk: e.matmul(
                               bank(m),
                               A[:, CAT0 + kk * 1024 + m * 128:CAT0 + kk * 1024 + (m + 1) * 128],
                               Bb[:, WOBF4 + b * 1024 + sub * 512:WOBF4 + b * 1024 + (sub + 1) * 512],
                               start=(kk == 0), stop=(kk == KD - 1)),
                           waits=wmm if (kk == 0 or m == 0) else (),
                           ms=("womm", nn, k2) if (m == 7 and sub == 1) else None)
            for m in range(8):
                zb = m % 4
                xb = m % 4
                wx = [p2done]
                if nn > 0 or m >= 4:
                    pm = (nn, m - 4) if m >= 4 else (nn - 1, m + 4)
                    wx.append(("zev",) + pm)
                dma("sync", Bf[:, XRS4 + xb * 512:XRS4 + (xb + 1) * 512],
                    xres_d[m * 128:(m + 1) * 128, nn * 512:(nn + 1) * 512],
                    waits=wx, ms=("xrsd", nn, m))
                w = [("womm", nn, 15), ("xrsd", nn, m)]
                if nn > 0 or m >= 4:
                    pm = (nn, m - 4) if m >= 4 else (nn - 1, m + 4)
                    w.append(("zdma",) + pm)
                op("vector",
                   lambda e, m=m, zb=zb, xb=xb: e.tensor_add(
                       Bf[:, ZST4 + zb * 512:ZST4 + (zb + 1) * 512], bank(m),
                       Bf[:, XRS4 + xb * 512:XRS4 + (xb + 1) * 512]),
                   waits=w, ms=("zev", nn, m))
                op("vector",
                   lambda e, m=m, nn=nn, zb=zb: e.bn_stats(
                       stats[:, ST_BN + (m * 8 + nn) * 6:ST_BN + (m * 8 + nn + 1) * 6],
                       Bf[:, ZST4 + zb * 512:ZST4 + (zb + 1) * 512]),
                   waits=[("zev", nn, m)], ms=("bnst", m, nn))
                dma("gpsimd", z_hbm[m][:, nn * 512:(nn + 1) * 512],
                    Bf[:, ZST4 + zb * 512:ZST4 + (zb + 1) * 512],
                    waits=[("zev", nn, m), ("bnst", m, nn)], ms=("zdma", nn, m))

        zall_list = [("zdma", 7, i) for i in range(8)]

        TSL5, X2F5, X2B5 = 0, 2048, 6144
        for m in range(8):
            op("vector",
               lambda e, m=m: e.bn_aggr(stats[:, ST_MV + m * 2:ST_MV + (m + 1) * 2],
                                        stats[:, ST_BN + m * 48:ST_BN + (m + 1) * 48]),
               waits=[("bnst", m, n2) for n2 in range(8)], ms=("mv", m))
            op("scalar",
               lambda e, m=m: e.activation(stats[:, ST_RSTD + m:ST_RSTD + m + 1],
                                           stats[:, ST_MV + m * 2 + 1:ST_MV + m * 2 + 2],
                                           AF.Ln, bias=stats[:, ST_EPS:ST_EPS + 1], scale=1.0),
               waits=[("mv", m), ("epsv",)], ms=("lnv", m))
            op("scalar",
               lambda e, m=m: e.activation(stats[:, ST_RSTD + m:ST_RSTD + m + 1],
                                           stats[:, ST_RSTD + m:ST_RSTD + m + 1],
                                           AF.Exp, bias=0.0, scale=-0.5),
               waits=[("lnv", m)], ms=("rstd", m))
            op("vector",
               lambda e, m=m: e.tensor_mul(stats[:, ST_NMR + m:ST_NMR + m + 1],
                                           stats[:, ST_MV + m * 2:ST_MV + m * 2 + 1],
                                           stats[:, ST_RSTD + m:ST_RSTD + m + 1]),
               waits=[("rstd", m)], ms=("nmr", m))
            for nn in range(8):
                b = nn % 4
                w = list(zall_list) + [("zdma", 7, m)]
                if nn >= 4:
                    w.append(("x2n", m, nn - 4))
                elif m > 0:
                    w.append(("x2n", m - 1, nn + 4))
                dma("sync", Bf[:, TSL5 + b * 512:TSL5 + (b + 1) * 512],
                    z_hbm[m][:, nn * 512:(nn + 1) * 512], waits=w, ms=("zrd", m, nn))
                tb = nn % 2
                wn = [("zrd", m, nn), ("nmr", m)]
                if nn >= 2:
                    wn.append(("x2dma", m, nn - 2))
                elif m > 0:
                    wn.append(("x2dma", m - 1, nn + 6))
                op("vector",
                   lambda e, m=m, b=b, tb=tb: e.tensor_scalar(
                       Bf[:, X2F5 + tb * 512:X2F5 + (tb + 1) * 512],
                       Bf[:, TSL5 + b * 512:TSL5 + (b + 1) * 512],
                       stats[:, ST_RSTD + m:ST_RSTD + m + 1],
                       stats[:, ST_NMR + m:ST_NMR + m + 1],
                       ALU.mult, ALU.subtract),
                   waits=wn, ms=("x2n", m, nn))
                dma("gpsimd", x2_hbm[m][:, nn * 512:(nn + 1) * 512],
                    Bf[:, X2F5 + tb * 512:X2F5 + (tb + 1) * 512],
                    waits=[("x2n", m, nn)], ms=("x2dma", m, nn))
                wxc = [("x2n", m, nn)]
                if nn >= 2:
                    wxc.append(("x2t", m, (nn - 2) * 4 + 3))
                elif m > 0:
                    wxc.append(("x2t", m - 1, (nn + 6) * 4 + 3))
                op("scalar",
                   lambda e, tb=tb: e.copy(
                       Bb[:, X2B5 + tb * 512:X2B5 + (tb + 1) * 512],
                       Bf[:, X2F5 + tb * 512:X2F5 + (tb + 1) * 512]),
                   waits=wxc, ms=("x2c", m, nn))
                wrsp = [("x2c", m, nn)]
                if m > 0 and nn == 0:
                    wrsp.append(("rs", m - 1))
                op("vector",
                   lambda e, nn=nn, tb=tb: e.reduce_sum(
                       stats[:, ST_RSP + nn:ST_RSP + nn + 1],
                       Bb[:, X2B5 + tb * 512:X2B5 + (tb + 1) * 512], axis=AX.X),
                   waits=wrsp, ms=("rsp", m, nn))
                pbk = len(grp_hist) % 2
                wtr = [("x2c", m, nn)]
                if len(grp_hist) >= 2:
                    wtr.append(grp_hist[-2])
                for t4 in range(4):
                    kkc = nn * 4 + t4
                    op("tensor",
                       lambda e, pbk=pbk, tb=tb, t4=t4: e.transpose(
                           trP[pbk][:, t4 * 128:(t4 + 1) * 128],
                           Bb[:, X2B5 + tb * 512 + t4 * 128:X2B5 + tb * 512 + (t4 + 1) * 128],
                           identb[:, :]),
                       waits=wtr if t4 == 0 else (), ms=("x2t", m, kkc))
                gkey = ("x2g", m, nn)
                op("vector",
                   lambda e, pbk=pbk, m=m, nn=nn: e.tensor_copy(
                       A.rearrange("p (c t) -> p c t", t=1024)[:, 32 + nn * 4:36 + nn * 4,
                                                              m * 128:(m + 1) * 128],
                       trP[pbk][:, 0:512].rearrange("p (c t) -> p c t", t=128)),
                   waits=[("x2t", m, nn * 4 + 3)], ms=gkey)
                grp_hist.append(gkey)
            op("vector",
               lambda e, m=m: e.reduce_sum(stats[:, ST_RS + m:ST_RS + m + 1],
                                           stats[:, ST_RSP:ST_RSP + 8], axis=AX.X),
               waits=[("rsp", m, n2) for n2 in range(8)], ms=("rs", m))
        op("vector",
           lambda e: e.tensor_scalar_mul(stats[:, ST_NB:ST_NB + 8], stats[:, ST_RS:ST_RS + 8],
                                         stats[:, ST_NEGFS:ST_NEGFS + 1]),
           waits=[("rs", m2) for m2 in range(8)] + [("negfs",)], ms=("nb",))

        x2tall = ("x2g", 7, 7)

        FWST6 = 0
        FWBF6 = 1024
        X2RS6 = 3072
        Z2ST6 = 5120
        for nn in range(8):
            for k2 in range(16):
                b = k2 % 2
                if k2 >= 2:
                    w = [("fwcast", nn, k2 - 2)]
                elif nn > 0:
                    w = [("fwcast", nn - 1, 14 + k2)]
                else:
                    w = [("nb",), x2tall]
                dma("sync", Bu[:, FWST6 + b * 1024:FWST6 + (b + 1) * 1024].rearrange("p (c m) -> p c m", m=512),
                    fw_d[k2 * 256:(k2 + 1) * 256, nn * 512:(nn + 1) * 512].rearrange("(c p) m -> p c m", p=128),
                    waits=w, ms=("fwd", nn, k2))
                wc = [("fwd", nn, k2)]
                if k2 >= 2:
                    wc.append(("fmm", nn, k2 - 2))
                op("scalar",
                   lambda e, b=b: e.copy(Bb[:, FWBF6 + b * 1024:FWBF6 + (b + 1) * 1024],
                                         Bu[:, FWST6 + b * 1024:FWST6 + (b + 1) * 1024]),
                   waits=wc, ms=("fwcast", nn, k2))
                for sub in range(2):
                    kk = 2 * k2 + sub
                    for m in range(8):
                        wmm = [("fwcast", nn, k2), x2tall]
                        if kk == 0 and nn > 0:
                            wmm.append(("z2ev", nn - 1, m))
                        op("tensor",
                           lambda e, m=m, b=b, sub=sub, kk=kk: e.matmul(
                               bank(m),
                               A[:, X2T0 + kk * 1024 + m * 128:X2T0 + kk * 1024 + (m + 1) * 128],
                               Bb[:, FWBF6 + b * 1024 + sub * 512:FWBF6 + b * 1024 + (sub + 1) * 512],
                               start=(kk == 0), stop=(kk == KD - 1)),
                           waits=wmm if (kk == 0 or m == 0) else (),
                           ms=("fmm", nn, k2) if (m == 7 and sub == 1) else None)
            for m in range(8):
                zb = m % 4
                xb = m % 4
                wx = [("nb",), x2tall]
                if nn > 0 or m >= 4:
                    pm = (nn, m - 4) if m >= 4 else (nn - 1, m + 4)
                    wx.append(("z2ev",) + pm)
                dma("sync", Bf[:, X2RS6 + xb * 512:X2RS6 + (xb + 1) * 512],
                    x2_hbm[m][:, nn * 512:(nn + 1) * 512],
                    waits=wx + [("x2dma", m, nn)], ms=("x2sl", nn, m))
                op("vector",
                   lambda e, m=m, xb=xb: e.tensor_scalar_add(
                       Bf[:, X2RS6 + xb * 512:X2RS6 + (xb + 1) * 512],
                       Bf[:, X2RS6 + xb * 512:X2RS6 + (xb + 1) * 512],
                       stats[:, ST_NB + m:ST_NB + m + 1]),
                   waits=[("x2sl", nn, m), ("nb",)], ms=("xadj", nn, m))
                w = [("fmm", nn, 15), ("xadj", nn, m), ("x2sl", nn, m), ("fs2",)]
                if nn > 0 or m >= 4:
                    pm = (nn, m - 4) if m >= 4 else (nn - 1, m + 4)
                    w.append(("z2dma",) + pm)
                op("vector",
                   lambda e, m=m, zb=zb, xb=xb: e.scalar_tensor_tensor(
                       Bf[:, Z2ST6 + zb * 512:Z2ST6 + (zb + 1) * 512], bank(m),
                       stats[:, ST_FS2:ST_FS2 + 1],
                       Bf[:, X2RS6 + xb * 512:X2RS6 + (xb + 1) * 512],
                       ALU.mult, ALU.add),
                   waits=w, ms=("z2ev", nn, m))
                wb = [("z2ev", nn, m)]
                if nn == 0:
                    wb.append(("mv", 7))
                op("vector",
                   lambda e, m=m, nn=nn, zb=zb: e.bn_stats(
                       stats[:, ST_BN + (m * 8 + nn) * 6:ST_BN + (m * 8 + nn + 1) * 6],
                       Bf[:, Z2ST6 + zb * 512:Z2ST6 + (zb + 1) * 512]),
                   waits=wb, ms=("bnst2", m, nn))
                dma("gpsimd", z2_hbm[m][:, nn * 512:(nn + 1) * 512],
                    Bf[:, Z2ST6 + zb * 512:Z2ST6 + (zb + 1) * 512],
                    waits=[("bnst2", m, nn)], ms=("z2dma", nn, m))

        z2all_list = [("z2dma", 7, i) for i in range(8)]

        T7, O7 = 0, 2048
        for m in range(8):
            op("vector",
               lambda e, m=m: e.bn_aggr(stats[:, ST_MV + m * 2:ST_MV + (m + 1) * 2],
                                        stats[:, ST_BN + m * 48:ST_BN + (m + 1) * 48]),
               waits=[("bnst2", m, n2) for n2 in range(8)] + [("rstd", 7), ("nmr", 7)],
               ms=("mv2", m))
            op("scalar",
               lambda e, m=m: e.activation(stats[:, ST_RSTD + m:ST_RSTD + m + 1],
                                           stats[:, ST_MV + m * 2 + 1:ST_MV + m * 2 + 2],
                                           AF.Ln, bias=stats[:, ST_EPS:ST_EPS + 1], scale=1.0),
               waits=[("mv2", m), ("epsv",)], ms=("lnv2", m))
            op("scalar",
               lambda e, m=m: e.activation(stats[:, ST_RSTD + m:ST_RSTD + m + 1],
                                           stats[:, ST_RSTD + m:ST_RSTD + m + 1],
                                           AF.Exp, bias=0.0, scale=-0.5),
               waits=[("lnv2", m)], ms=("rstd2", m))
            op("vector",
               lambda e, m=m: e.tensor_mul(stats[:, ST_NMR + m:ST_NMR + m + 1],
                                           stats[:, ST_MV + m * 2:ST_MV + m * 2 + 1],
                                           stats[:, ST_RSTD + m:ST_RSTD + m + 1]),
               waits=[("rstd2", m)], ms=("nmr2", m))
            for nn in range(8):
                b = nn % 4
                w = list(z2all_list) + [("z2dma", 7, m)]
                if nn >= 4:
                    w.append(("outn", m, nn - 4))
                elif m > 0:
                    w.append(("outn", m - 1, nn + 4))
                dma("sync", Bf[:, T7 + b * 512:T7 + (b + 1) * 512],
                    z2_hbm[m][:, nn * 512:(nn + 1) * 512], waits=w, ms=("z2rd", m, nn))
                tb = nn % 2
                wn = [("z2rd", m, nn), ("nmr2", m)]
                if nn >= 2:
                    wn.append(("outdma", m, nn - 2))
                elif m > 0:
                    wn.append(("outdma", m - 1, nn + 6))
                op("vector",
                   lambda e, m=m, b=b, tb=tb: e.tensor_scalar(
                       Bf[:, O7 + tb * 512:O7 + (tb + 1) * 512],
                       Bf[:, T7 + b * 512:T7 + (b + 1) * 512],
                       stats[:, ST_RSTD + m:ST_RSTD + m + 1],
                       stats[:, ST_NMR + m:ST_NMR + m + 1],
                       ALU.mult, ALU.subtract),
                   waits=wn, ms=("outn", m, nn))
                dma("gpsimd", out_d[m * 128:(m + 1) * 128, nn * 512:(nn + 1) * 512],
                    Bf[:, O7 + tb * 512:O7 + (tb + 1) * 512],
                    waits=[("outn", m, nn)], ms=("outdma", m, nn))

        prev = ("outdma", 7, 7)
        if "qkv" in taps:
            dma("sync", tap_outs["tap_q"], qspill, waits=[prev], ms=("tapq",))
            dma("sync", tap_outs["tap_k"], kspill, waits=[("tapq",)], ms=("tapk",))
            dma("sync", tap_outs["tap_v"], vspill, waits=[("tapk",)], ms=("tapv",))
        if "cat" in taps:
            dma("sync", tap_outs["tap_cat"], A[:, 0:32768], waits=[prev], ms=("tapcat",))
        if "z" in taps:
            dma("sync", tap_outs["tap_z"], z_hbm, waits=[prev], ms=("tapz",))
        if "x2" in taps:
            dma("sync", tap_outs["tap_x2"], x2_hbm, waits=[prev], ms=("tapx2",))
        if "z2" in taps:
            dma("sync", tap_outs["tap_z2"], z2_hbm, waits=[prev], ms=("tapz2",))

        for i in range(QK["sync"]):
            E.final_wait("sync", f"s{i}")
        for i in range(QK["gpsimd"]):
            E.final_wait("gpsimd", f"g{i}")

    E.begin_count()
    walk()

    sems = {}
    with ExitStack() as semstack:
        for name in (["t", "v", "s", "p"] + [f"s{i}" for i in range(12)] + [f"g{i}" for i in range(8)]):
            sems[name] = semstack.enter_context(nc.semaphore(f"sem_{name}"))
        with nc.Block() as block:
            @block.sync
            def _(eng):
                E.begin_emit("sync", eng, sems)
                walk()

            @block.gpsimd
            def _(eng):
                E.begin_emit("gpsimd", eng, sems)
                walk()

            @block.tensor
            def _(eng):
                E.begin_emit("tensor", eng, sems)
                walk()

            @block.vector
            def _(eng):
                E.begin_emit("vector", eng, sems)
                walk()

            @block.scalar
            def _(eng):
                E.begin_emit("scalar", eng, sems)
                walk()
    es.close()
    return nc



def _rot_perm():
    p = np.arange(D)
    for h in range(NH):
        base = h * HD
        p[base:base + 64] = np.concatenate([np.arange(0, 64, 2), np.arange(1, 64, 2)]) + base
    return p


def _tables(idx):
    inv_freq = (1.0 / (10000.0 ** (np.arange(0, 64, 2, dtype=np.float32) / np.float32(64.0)))).astype(np.float32)
    t = idx.astype(np.float32)
    sinus = np.einsum("i,j->ij", t, inv_freq).astype(np.float32)
    sn = np.sin(sinus).astype(np.float32).T
    cs = np.cos(sinus).astype(np.float32).T
    out = np.zeros((128, len(idx)), np.float32)
    out[0:32] = cs
    out[32:64] = cs
    out[64:96] = sn
    out[96:128] = sn
    return out


def _in_maps(x, wq, wk, wv, wo, fw, fs):
    ident = np.eye(128, dtype=np.float32).astype(BFNP)
    mval = np.float32(NEGV / 16.0)
    tri = np.where(np.arange(128)[:, None] >= np.arange(128)[None, :],
                   np.float32(0.0), mval).astype(BFNP)
    mb = [np.full((128, 128), mval, np.float32).astype(BFNP),
          np.zeros((128, 128), np.float32).astype(BFNP)]
    maps, myidx_all = [], []
    for c in range(8):
        b, par = c // 2, c % 2
        myidx = np.concatenate([np.arange((2 * jj + par) * 128, (2 * jj + par + 1) * 128)
                                for jj in range(8)])
        pidx = np.concatenate([np.arange((2 * jj + 1 - par) * 128, (2 * jj + 2 - par) * 128)
                               for jj in range(8)])
        allidx = np.concatenate([myidx, pidx])
        myidx_all.append(myidx)
        maps.append({
            "xT": np.ascontiguousarray(x[b].T[:, allidx]),
            "xres": np.ascontiguousarray(x[b][myidx]),
            "wq": wq, "wk": wk, "wv": wv, "wo": wo, "fw": fw, "fs": fs,
            "tabq": _tables(myidx).astype(BFNP), "tabk": _tables(allidx).astype(BFNP),
            "maska": tri, "maskb": mb[par], "ident": ident,
        })
    return maps, myidx_all


def kernel(L0_attn_hidden_states, L0_attn_Wq, L0_attn_Wk, L0_attn_Wv, L0_attn_Wo,
           L0_ffn_kernel, L0_ffn_scale):
    x = np.asarray(L0_attn_hidden_states)
    perm = _rot_perm()
    wq = np.ascontiguousarray(np.asarray(L0_attn_Wq)[:, perm])
    wk = np.ascontiguousarray(np.asarray(L0_attn_Wk)[:, perm])
    wv = np.ascontiguousarray(np.asarray(L0_attn_Wv))
    wo = np.ascontiguousarray(np.asarray(L0_attn_Wo))
    fw = np.ascontiguousarray(np.asarray(L0_ffn_kernel).view(np.uint8))
    fs = np.asarray(L0_ffn_scale, dtype=np.float32).reshape(1, 1)

    nc = build_nc()
    in_maps, myidx_all = _in_maps(x, wq, wk, wv, wo, fw, fs)
    res = run_bass_kernel_spmd(nc, in_maps, core_ids=list(range(8)))
    out = np.empty((4, S, D), np.float32)
    for c in range(8):
        out[c // 2, myidx_all[c], :] = res.results[c]["out"]
    return out


# version 10
# speedup vs baseline: 1.2339x; 1.2339x over previous
from contextlib import ExitStack

import numpy as np
import ml_dtypes

import concourse.bass as bass
import concourse.mybir as mybir
from concourse.bass_utils import run_bass_kernel_spmd

F32 = mybir.dt.float32
BF16 = mybir.dt.bfloat16
U8 = mybir.dt.uint8
U32 = mybir.dt.uint32
AF = mybir.ActivationFunctionType
ALU = mybir.AluOpType
AX = mybir.AxisListType

D = 4096
S = 2048
NH = 16
HD = 256
KD = D // 128
ACUT = 10
EPS = 1e-5
NEGV = -1e9
SCL = 1.0 / 16.0

BFNP = np.dtype(ml_dtypes.bfloat16)


def _split512(n):
    out = []
    while n > 0:
        c = min(n, 512)
        out.append(c)
        n -= c
    return out


class Sched:
    def __init__(self):
        self.mode = "count"
        self.cnt = {}
        self.ms = {}
        self.eng = None
        self.engname = None
        self.sems = None
        self.waited = None
        self.fence = {}

    def begin_count(self):
        self.mode = "count"
        self.cnt = {}
        self.ms = {}
        self.fence = {}

    def begin_emit(self, engname, eng, sems):
        self.mode = "emit"
        self.engname = engname
        self.eng = eng
        self.sems = sems
        self.waited = {}
        self.fence = {}

    def set_fence(self, keys):
        for en in ("sync", "gpsimd", "tensor", "vector", "scalar"):
            self.fence[en] = list(keys)

    def op(self, engname, fn, waits=(), inc=None, ms=None):
        pend = self.fence.pop(engname, None)
        if pend:
            waits = list(waits) + pend
        if self.mode == "count":
            for k in waits:
                assert k in self.ms, f"wait on undefined milestone {k}"
            if inc is not None:
                s, n = inc
                self.cnt[s] = self.cnt.get(s, 0) + n
                if ms is not None:
                    for k in (ms if isinstance(ms, list) else [ms]):
                        assert k not in self.ms, f"dup milestone {k}"
                        self.ms[k] = (s, self.cnt[s])
            else:
                assert ms is None, ms
            return
        if engname != self.engname:
            return
        for k in waits:
            s, v = self.ms[k]
            if v > self.waited.get(s, 0):
                self.eng.wait_ge(self.sems[s], v)
                self.waited[s] = v
        instr = fn(self.eng)
        if inc is not None:
            s, n = inc
            instr.then_inc(self.sems[s], n)

    def final_wait(self, engname, semname):
        if self.mode == "count" or engname != self.engname:
            return
        v = self.cnt.get(semname, 0)
        if v > self.waited.get(semname, 0):
            self.eng.wait_ge(self.sems[semname], v)
            self.waited[semname] = v


def build_nc():
    nc = bass.Bass("TRN2", target_bir_lowering=False)

    NQR = [ACUT, NH - ACUT]
    NKVR = [ACUT, 16]
    QB0R = [0, ACUT]
    NMX = max(NQR)

    xT_d = nc.dram_tensor("xT", [D, S], F32, kind="ExternalInput").ap()
    xres_d = nc.dram_tensor("xres", [NMX * 128, D], F32, kind="ExternalInput").ap()
    wq_d = nc.dram_tensor("wq", [D, D], F32, kind="ExternalInput").ap()
    wk_d = nc.dram_tensor("wk", [D, D], F32, kind="ExternalInput").ap()
    wv_d = nc.dram_tensor("wv", [D, D], F32, kind="ExternalInput").ap()
    wo_d = nc.dram_tensor("wo", [D, D], F32, kind="ExternalInput").ap()
    fw_d = nc.dram_tensor("fw", [D, D], U8, kind="ExternalInput").ap()
    fs_d = nc.dram_tensor("fs", [1, 1], F32, kind="ExternalInput").ap()
    role_d = nc.dram_tensor("rolein", [1, 1], U32, kind="ExternalInput").ap()
    tabq_d = nc.dram_tensor("tabq", [128, NMX * 128], BF16, kind="ExternalInput").ap()
    tabk_d = nc.dram_tensor("tabk", [128, S], BF16, kind="ExternalInput").ap()
    maska_d = nc.dram_tensor("maska", [128, 128], BF16, kind="ExternalInput").ap()
    ident_d = nc.dram_tensor("ident", [128, 128], BF16, kind="ExternalInput").ap()
    out_d = nc.dram_tensor("out", [NMX * 128, D], F32, kind="ExternalOutput").ap()

    qspill = nc.dram_tensor("qspill", [NH, HD, NMX * 128], BF16, kind="Internal").ap()
    kspill = nc.dram_tensor("kspill", [NH, HD, S], BF16, kind="Internal").ap()
    vspill = nc.dram_tensor("vspill", [NH, 128, 16 * HD], BF16, kind="Internal").ap()
    z_hbm = nc.dram_tensor("z_hbm", [NMX, 128, D], F32, kind="Internal").ap()
    x2_hbm = nc.dram_tensor("x2_hbm", [NMX, 128, D], F32, kind="Internal").ap()
    z2_hbm = nc.dram_tensor("z2_hbm", [NMX, 128, D], F32, kind="Internal").ap()

    es = ExitStack()
    A = es.enter_context(nc.sbuf_tensor("arenaA", [128, 65536], BF16))
    Bu = es.enter_context(nc.sbuf_tensor("arenaB", [128, 64512], U8))
    Bf = Bu.bitcast(F32)
    Bb = Bu.bitcast(BF16)
    identb = es.enter_context(nc.sbuf_tensor("identb", [128, 128], BF16))
    maska = es.enter_context(nc.sbuf_tensor("maska_s", [128, 128], BF16))
    tabqc = es.enter_context(nc.sbuf_tensor("tabqc", [64, NMX * 128], BF16))
    tabqs = es.enter_context(nc.sbuf_tensor("tabqs", [64, NMX * 128], BF16))
    tabkc = es.enter_context(nc.sbuf_tensor("tabkc", [64, S], BF16))
    tabks = es.enter_context(nc.sbuf_tensor("tabks", [64, S], BF16))
    stats = es.enter_context(nc.sbuf_tensor("stats", [128, 576], F32))
    ones1 = es.enter_context(nc.sbuf_tensor("ones1", [1, 128], F32))
    rolesb = es.enter_context(nc.sbuf_tensor("rolesb", [1, 1], U32))

    ST_BN = 0
    ST_MV = 480
    ST_RSTD = 500
    ST_NMR = 512
    ST_RS = 524
    ST_NB = 536
    ST_FS = 548
    ST_FSB = 549
    ST_FS2 = 550
    ST_NEGFS = 551
    ST_EPS = 552
    ST_RSP = 560

    PL = es.enter_context(nc.psum_tensor("PL", [128, 2048], F32))
    pb = [es.enter_context(nc.psum_tensor(f"pb{i}", [128, 512], F32)) for i in range(4)]
    trP = [p.bitcast(BF16) for p in pb[:2]]
    avP = [pb[2], pb[3]]

    def bank(i):
        if i < 4:
            return PL[:, i * 512:(i + 1) * 512]
        return pb[i - 4][:, :]

    E0, E1 = Sched(), Sched()

    def walk(role, E):
        NQ = NQR[role]
        NKV = NKVR[role]
        QB0 = QB0R[role]
        NM = NQ
        QW = NQ * 128
        KWA = NKV * 128
        T2 = NH * NQ

        QK = {"sync": 12, "gpsimd": 8}
        qc = {"sync": 0, "gpsimd": 0}

        def dma(q, out, in_, waits=(), ms=None):
            j = qc[q]
            qc[q] = j + 1
            sem = f"{q[0]}{j % QK[q]}"
            w = list(waits)
            if j >= QK[q]:
                w.append(("_dma", q, j - QK[q]))
            mss = [("_dma", q, j)] + ([ms] if ms is not None else [])
            E.op(q, lambda e, o=out, i=in_: e.dma_start(out=o, in_=i),
                 w, inc=(sem, 16), ms=mss)

        def op(eng, fn, waits=(), ms=None):
            semname = {"tensor": "t", "vector": "v", "scalar": "s", "gpsimd": "p"}[eng]
            E.op(eng, fn, waits, inc=(semname, 1) if ms is not None else None, ms=ms)

        def dma_tail(q):
            j1 = qc[q]
            return [("_dma", q, j) for j in range(max(0, j1 - QK[q]), j1)]

        bankfree = [None] * 8

        def bwait(i):
            return [bankfree[i]] if bankfree[i] is not None else []

        dma("sync", identb[:, :], ident_d, ms=("d_ident",))
        dma("sync", maska[:, :], maska_d, ms=("d_maska",))
        dma("sync", tabqc[:, :], tabq_d[0:64, :], ms=("tabq",))
        dma("sync", tabqs[:, :], tabq_d[64:128, :], ms=("tabqs",))
        dma("sync", tabkc[:, :], tabk_d[0:64, :], ms=("tabk",))
        dma("sync", tabks[:, :], tabk_d[64:128, :], ms=("tabks",))
        dma("sync", stats[0:1, ST_FS:ST_FS + 1], fs_d, ms=("d_fs",))
        op("vector", lambda e: e.memset(ones1[:, :], 1.0), ms=("ones",))
        op("vector", lambda e: e.memset(stats[:, ST_EPS:ST_EPS + 1], float(EPS)), ms=("epsv",))
        op("tensor", lambda e: e.matmul(pb[0][:, 0:1], ones1[0:1, :],
                                        stats[0:1, ST_FS:ST_FS + 1], start=True, stop=True),
           waits=[("ones",), ("d_fs",)], ms=("fsmm",))
        op("vector", lambda e: e.tensor_copy(stats[:, ST_FSB:ST_FSB + 1], pb[0][:, 0:1]),
           waits=[("fsmm",)], ms=("fsb",))
        op("vector", lambda e: e.tensor_scalar_mul(stats[:, ST_FS2:ST_FS2 + 1],
                                                   stats[:, ST_FSB:ST_FSB + 1], 2.0),
           waits=[("fsb",)], ms=("fs2",))
        op("vector", lambda e: e.tensor_scalar_mul(stats[:, ST_NEGFS:ST_NEGFS + 1],
                                                   stats[:, ST_FSB:ST_FSB + 1], -1.0),
           waits=[("fsb",)], ms=("negfs",))
        bankfree[4] = ("fsb",)

        XSTb = 0
        WQSTb = XSTb + 2 * KWA * 4
        WKSTb = WQSTb + 2 * 512 * 4
        WVSTb = WKSTb + 2 * 512 * 4
        WQBFb = WVSTb + 2 * 1024 * 4
        WKBFb = WQBFb + 2 * 512 * 2
        WVBFb = WKBFb + 2 * 512 * 2
        QSTGb = WVBFb + 2 * 1024 * 2
        KSTGb = QSTGb + 2 * QW * 2
        VSTGb = KSTGb + 2 * KWA * 2
        assert VSTGb + 4096 * 2 <= 64512, VSTGb
        XST = XSTb // 4
        WQST, WKST, WVST = WQSTb // 4, WKSTb // 4, WVSTb // 4
        WQBF, WKBF, WVBF = WQBFb // 2, WKBFb // 2, WVBFb // 2
        QSTG, KSTG, VSTG = QSTGb // 2, KSTGb // 2, VSTGb // 2

        qsplit = _split512(QW)
        ksplit = _split512(KWA)
        nqc, nkc = len(qsplit), len(ksplit)
        vpasses = []
        c0 = 0
        while c0 < NKV:
            vpasses.append((c0, min(c0 + 8, NKV)))
            c0 += 8
        NVP = len(vpasses)

        for kk in range(KD):
            b = kk % 2
            w = [("xcast", kk - 2)] if kk >= 2 else []
            dma("gpsimd", Bf[:, XST + b * KWA:XST + (b + 1) * KWA],
                xT_d[kk * 128:(kk + 1) * 128, 0:KWA],
                waits=w, ms=("xdma", kk))
            op("scalar",
               lambda e, kk=kk, b=b: e.copy(
                   A[:, kk * KWA:(kk + 1) * KWA],
                   Bf[:, XST + b * KWA:XST + (b + 1) * KWA]),
               waits=[("xdma", kk)], ms=("xcast", kk))

        for p in range(8):
            h0 = 2 * p
            for hh in range(2):
                h = h0 + hh
                for k2 in range(16):
                    b = k2 % 2
                    if k2 >= 2:
                        w = [("qcast", p, hh, k2 - 2)]
                    elif hh == 1:
                        w = [("qcast", p, 0, 14 + k2)]
                    elif p > 0:
                        w = [("qcast", p - 1, 1, 14 + k2)]
                    else:
                        w = []
                    dma("sync", Bf[:, WQST + b * 512:WQST + (b + 1) * 512].rearrange("p (c m) -> p c m", m=256),
                        wq_d[k2 * 256:(k2 + 1) * 256, h * HD:(h + 1) * HD].rearrange("(c p) m -> p c m", p=128),
                        waits=w, ms=("qwd", p, hh, k2))
                    wc = [("qwd", p, hh, k2)]
                    if k2 >= 2:
                        wc.append(("qmm", p, hh, k2 - 2))
                    op("vector",
                       lambda e, b=b: e.tensor_copy(Bb[:, WQBF + b * 512:WQBF + (b + 1) * 512],
                                                    Bf[:, WQST + b * 512:WQST + (b + 1) * 512]),
                       waits=wc, ms=("qcast", p, hh, k2))
                    for sub in range(2):
                        kk = 2 * k2 + sub
                        coff = 0
                        for n, csz in enumerate(qsplit):
                            for m in range(2):
                                wmm = [("qcast", p, hh, k2), ("xcast", kk)]
                                if kk == 0:
                                    wmm += bwait(m * nqc + n)
                                op("tensor",
                                   lambda e, b=b, sub=sub, m=m, n=n, kk=kk, coff=coff, csz=csz: e.matmul(
                                       bank(m * nqc + n)[:, 0:csz],
                                       Bb[:, WQBF + b * 512 + sub * 256 + m * 128:
                                          WQBF + b * 512 + sub * 256 + (m + 1) * 128],
                                       A[:, kk * KWA + QB0 * 128 + coff:kk * KWA + QB0 * 128 + coff + csz],
                                       start=(kk == 0), stop=(kk == KD - 1)),
                                   waits=wmm if (kk == 0 or (m == 0 and n == 0)) else (),
                                   ms=("qmm", p, hh, k2) if (m == 1 and n == nqc - 1 and sub == 1) else None)
                            coff += csz
                for m in range(2):
                    coff = 0
                    for n, csz in enumerate(qsplit):
                        w = [("qmm", p, hh, 15)]
                        if hh == 1:
                            w.append(("qspill", p, 0))
                        elif p > 0:
                            w.append(("qspill", p - 1, 1))
                        mskey = ("qevb", p, hh, m * nqc + n)
                        op("vector",
                           lambda e, m=m, coff=coff, csz=csz, n=n: e.tensor_scalar_mul(
                               Bb[:, QSTG + m * QW + coff:QSTG + m * QW + coff + csz],
                               bank(m * nqc + n)[:, 0:csz], SCL),
                           waits=w, ms=mskey)
                        bankfree[m * nqc + n] = mskey
                        coff += csz
                dma("gpsimd",
                    qspill[h][:, 0:QW].rearrange("(c p) t -> p c t", p=128),
                    Bb[:, QSTG:QSTG + 2 * QW].rearrange("p (c t) -> p c t", t=QW),
                    waits=[("qevb", p, hh, m * nqc + n) for m in range(2) for n in range(nqc)],
                    ms=("qspill", p, hh))

            for hh in range(2):
                h = h0 + hh
                for k2 in range(16):
                    b = k2 % 2
                    if k2 >= 2:
                        w = [("kcast", p, hh, k2 - 2)]
                    elif hh == 1:
                        w = [("kcast", p, 0, 14 + k2)]
                    elif p > 0:
                        w = [("kcast", p - 1, 1, 14 + k2)]
                    else:
                        w = []
                    dma("sync", Bf[:, WKST + b * 512:WKST + (b + 1) * 512].rearrange("p (c m) -> p c m", m=256),
                        wk_d[k2 * 256:(k2 + 1) * 256, h * HD:(h + 1) * HD].rearrange("(c p) m -> p c m", p=128),
                        waits=w, ms=("kwd", p, hh, k2))
                    wc = [("kwd", p, hh, k2)]
                    if k2 >= 2:
                        wc.append(("kmm", p, hh, k2 - 2))
                    op("vector",
                       lambda e, b=b: e.tensor_copy(Bb[:, WKBF + b * 512:WKBF + (b + 1) * 512],
                                                    Bf[:, WKST + b * 512:WKST + (b + 1) * 512]),
                       waits=wc, ms=("kcast", p, hh, k2))
                    for sub in range(2):
                        kk = 2 * k2 + sub
                        coff = 0
                        for n, csz in enumerate(ksplit):
                            for m in range(2):
                                wmm = [("kcast", p, hh, k2), ("xcast", kk)]
                                if kk == 0:
                                    wmm += bwait(m * nkc + n)
                                op("tensor",
                                   lambda e, b=b, sub=sub, m=m, n=n, kk=kk, coff=coff, csz=csz: e.matmul(
                                       bank(m * nkc + n)[:, 0:csz],
                                       Bb[:, WKBF + b * 512 + sub * 256 + m * 128:
                                          WKBF + b * 512 + sub * 256 + (m + 1) * 128],
                                       A[:, kk * KWA + coff:kk * KWA + coff + csz],
                                       start=(kk == 0), stop=(kk == KD - 1)),
                                   waits=wmm if (kk == 0 or (m == 0 and n == 0)) else (),
                                   ms=("kmm", p, hh, k2) if (m == 1 and n == nkc - 1 and sub == 1) else None)
                            coff += csz
                for m in range(2):
                    coff = 0
                    for n, csz in enumerate(ksplit):
                        w = [("kmm", p, hh, 15)]
                        if hh == 1:
                            w.append(("kspill", p, 0))
                        elif p > 0:
                            w.append(("kspill", p - 1, 1))
                        mskey = ("kevb", p, hh, m * nkc + n)
                        op("vector",
                           lambda e, m=m, coff=coff, csz=csz, n=n: e.tensor_copy(
                               Bb[:, KSTG + m * KWA + coff:KSTG + m * KWA + coff + csz],
                               bank(m * nkc + n)[:, 0:csz]),
                           waits=w, ms=mskey)
                        bankfree[m * nkc + n] = mskey
                        coff += csz
                dma("gpsimd",
                    kspill[h][:, 0:KWA].rearrange("(c p) t -> p c t", p=128),
                    Bb[:, KSTG:KSTG + 2 * KWA].rearrange("p (c t) -> p c t", t=KWA),
                    waits=[("kevb", p, hh, m * nkc + n) for m in range(2) for n in range(nkc)],
                    ms=("kspill", p, hh))

            for ps, (ca, cb) in enumerate(vpasses):
                npc = cb - ca
                for k2 in range(16):
                    b = k2 % 2
                    if k2 >= 2:
                        w = [("vcast", p, ps, k2 - 2)]
                    elif ps > 0:
                        w = [("vcast", p, ps - 1, 14 + k2)]
                    elif p > 0:
                        w = [("vcast", p - 1, NVP - 1, 14 + k2)]
                    else:
                        w = []
                    dma("sync", Bf[:, WVST + b * 1024:WVST + (b + 1) * 1024].rearrange("p (c m) -> p c m", m=512),
                        wv_d[k2 * 256:(k2 + 1) * 256, h0 * HD:h0 * HD + 512].rearrange("(c p) m -> p c m", p=128),
                        waits=w, ms=("vwd", p, ps, k2))
                    wc = [("vwd", p, ps, k2)]
                    if k2 >= 2:
                        wc.append(("vmm", p, ps, k2 - 2))
                    op("vector",
                       lambda e, b=b: e.tensor_copy(Bb[:, WVBF + b * 1024:WVBF + (b + 1) * 1024],
                                                    Bf[:, WVST + b * 1024:WVST + (b + 1) * 1024]),
                       waits=wc, ms=("vcast", p, ps, k2))
                    for sub in range(2):
                        kk = 2 * k2 + sub
                        for bidx in range(npc):
                            c = ca + bidx
                            wmm = [("vcast", p, ps, k2), ("xcast", kk)]
                            if kk == 0:
                                wmm += bwait(bidx)
                            op("tensor",
                               lambda e, b=b, sub=sub, c=c, bidx=bidx, kk=kk: e.matmul(
                                   bank(bidx),
                                   A[:, kk * KWA + c * 128:kk * KWA + (c + 1) * 128],
                                   Bb[:, WVBF + b * 1024 + sub * 512:WVBF + b * 1024 + (sub + 1) * 512],
                                   start=(kk == 0), stop=(kk == KD - 1)),
                               waits=wmm if (kk == 0 or bidx == 0) else (),
                               ms=("vmm", p, ps, k2) if (bidx == npc - 1 and sub == 1) else None)
                for bidx in range(npc):
                    w = [("vmm", p, ps, 15)]
                    if ps > 0:
                        w += [("vspill", p, ps - 1, 0), ("vspill", p, ps - 1, 1)]
                    elif p > 0:
                        w += [("vspill", p - 1, NVP - 1, 0), ("vspill", p - 1, NVP - 1, 1)]
                    mskey = ("vevb", p, ps, bidx)
                    op("vector",
                       lambda e, bidx=bidx: e.tensor_copy(
                           Bb[:, VSTG + bidx * 512:VSTG + (bidx + 1) * 512], bank(bidx)),
                       waits=w, ms=mskey)
                    bankfree[bidx] = mskey
                for hh in range(2):
                    h = h0 + hh
                    dma("gpsimd",
                        vspill[h].rearrange("p (c d) -> p c d", d=HD)[:, ca:cb, :],
                        Bb[:, VSTG:VSTG + npc * 512].rearrange("p (c two) -> p c two", two=512)[:, :, hh * HD:(hh + 1) * HD],
                        waits=[("vevb", p, ps, i) for i in range(npc)],
                        ms=("vspill", p, ps, hh))

        p1done = ("vspill", 7, NVP - 1, 1)
        p1all = [("vspill", 7, NVP - 1, 0), ("vspill", 7, NVP - 1, 1)]
        npc_last = vpasses[-1][1] - vpasses[-1][0]
        E.set_fence([("vmm", 7, NVP - 1, 15), ("vevb", 7, NVP - 1, npc_last - 1),
                     ("xcast", 31)] + dma_tail("sync") + dma_tail("gpsimd"))

        SETSZ = 2 * QW + 2 * KWA + NKV * 256
        SETA0 = 32 * QW
        ANW = (QB0 + NQ) * 128
        ANB = SETA0 + 2 * SETSZ
        assert ANB + 6 * ANW <= 65536, (ANB, ANW)
        SET3b = 0
        SWQb = SET3b + SETSZ * 2
        SWKb = SWQb + QW * 2
        SCQb = SWKb + KWA * 2
        SCKb = SCQb + QW * 2
        AUBb = SCKb + KWA * 2
        SVBb = AUBb + 3 * ANW * 2
        assert SVBb + 3 * 16 * 4 <= 64512, SVBb
        SET3 = SET3b // 2
        SWQ, SWK, SCQ, SCK = SWQb // 2, SWKb // 2, SCQb // 2, SCKb // 2
        AUB0 = AUBb // 2
        SVB0 = SVBb // 4

        grp_hist = []

        def qkvset(h):
            si = h % 3
            if si < 2:
                return A, SETA0 + si * SETSZ
            return Bb, SET3

        def emit_head_loads(h):
            AR, qb = qkvset(h)
            kb = qb + 2 * QW
            vb = kb + 2 * KWA
            wl = [("qspill", h // 2, h % 2), ("kspill", h // 2, h % 2)] + \
                 [("vspill", h // 2, psx, h % 2) for psx in range(NVP)] + p1all
            if h >= 3:
                wl.append(("avmm", (h - 3) * NQ + NQ - 1))
            dma("gpsimd", AR[:, qb:qb + 2 * QW].rearrange("p (c t) -> p c t", t=QW),
                qspill[h][:, 0:QW].rearrange("(c p) t -> p c t", p=128),
                waits=wl, ms=("ldq", h))
            dma("gpsimd", AR[:, kb:kb + 2 * KWA].rearrange("p (c t) -> p c t", t=KWA),
                kspill[h][:, 0:KWA].rearrange("(c p) t -> p c t", p=128),
                waits=wl, ms=("ldk", h))
            dma("gpsimd", AR[:, vb:vb + NKV * 256],
                vspill[h][:, 0:NKV * 256], waits=wl, ms=("ldv", h))
            wr = [("ldq", h)] + ([("ropeq", h - 1)] if h >= 1 else [])
            dma("gpsimd", Bb[0:32, SWQ:SWQ + QW], AR[32:64, qb:qb + QW], waits=wr, ms=("swq1", h))
            dma("gpsimd", Bb[32:64, SWQ:SWQ + QW], AR[0:32, qb:qb + QW], waits=wr, ms=("swq2", h))
            wr = [("ldk", h)] + ([("ropek", h - 1)] if h >= 1 else [])
            dma("gpsimd", Bb[0:32, SWK:SWK + KWA], AR[32:64, kb:kb + KWA], waits=wr, ms=("swk1", h))
            dma("gpsimd", Bb[32:64, SWK:SWK + KWA], AR[0:32, kb:kb + KWA], waits=wr, ms=("swk2", h))
            wv_ = [("swq1", h), ("swq2", h), ("tabq",), ("tabqs",)] + ([("ropeq", h - 1)] if h >= 1 else [])
            op("gpsimd", lambda e, AR=AR, qb=qb: e.tensor_mul(Bb[0:64, SCQ:SCQ + QW],
                                                              AR[0:64, qb:qb + QW], tabqc[:, 0:QW]),
               waits=wv_, ms=("rq1", h))
            op("gpsimd", lambda e: e.tensor_mul(Bb[0:64, SWQ:SWQ + QW],
                                                Bb[0:64, SWQ:SWQ + QW], tabqs[:, 0:QW]),
               waits=[("rq1", h)] + wv_, ms=("rq2", h))
            op("gpsimd", lambda e, AR=AR, qb=qb: e.tensor_add(AR[0:64, qb:qb + QW],
                                                              Bb[0:64, SCQ:SCQ + QW],
                                                              Bb[0:64, SWQ:SWQ + QW]),
               waits=[("rq1", h), ("rq2", h)], ms=("ropeq", h))
            wv_ = [("swk1", h), ("swk2", h), ("tabk",), ("tabks",)] + ([("ropek", h - 1)] if h >= 1 else [])
            op("gpsimd", lambda e, AR=AR, kb=kb: e.tensor_mul(Bb[0:64, SCK:SCK + KWA],
                                                              AR[0:64, kb:kb + KWA], tabkc[:, 0:KWA]),
               waits=wv_, ms=("rk1", h))
            op("gpsimd", lambda e: e.tensor_mul(Bb[0:64, SWK:SWK + KWA],
                                                Bb[0:64, SWK:SWK + KWA], tabks[:, 0:KWA]),
               waits=[("rk1", h)] + wv_, ms=("rk2", h))
            op("gpsimd", lambda e, AR=AR, kb=kb: e.tensor_add(AR[0:64, kb:kb + KWA],
                                                              Bb[0:64, SCK:SCK + KWA],
                                                              Bb[0:64, SWK:SWK + KWA]),
               waits=[("rk1", h), ("rk2", h)], ms=("ropek", h))

        def emit_lg(t):
            h, jl = divmod(t, NQ)
            klen = QB0 + jl + 1
            W = klen * 128
            AR, qb = qkvset(h)
            kb = qb + 2 * QW
            wlog = [("ropeq", h), ("ropek", h), ("d_maska",), ("d_ident",)]
            wlog.append(("exp", t - 1) if t > 0 else p1done)
            if t == 0:
                for i in range(4):
                    wlog += bwait(i)
            first = True
            coff = 0
            chunks = _split512(W)
            for ci, csz in enumerate(chunks):
                isdiag = (ci == len(chunks) - 1)
                for m in range(2):
                    stop = (m == 1 and not isdiag)
                    op("tensor",
                       lambda e, m=m, coff=coff, csz=csz, qb=qb, kb=kb, jl=jl,
                       stop=stop, AR=AR: e.matmul(
                           PL[:, coff:coff + csz],
                           AR[:, qb + m * QW + jl * 128:qb + m * QW + (jl + 1) * 128],
                           AR[:, kb + m * KWA + coff:kb + m * KWA + coff + csz],
                           start=(m == 0), stop=stop),
                       waits=wlog if first else ())
                    first = False
                if isdiag:
                    op("tensor",
                       lambda e, W=W: e.matmul(
                           PL[:, W - 128:W], identb[:, :], maska[:, :],
                           start=False, stop=True),
                       ms=("lgp", t))
                coff += csz

        def emit_mx(t):
            h, jl = divmod(t, NQ)
            W = (QB0 + jl + 1) * 128
            SVX = SVB0 + 16 * (t % 3)
            wmx = [("lgp", t)]
            if t >= 3:
                wmx.append(("exp", t - 3))
            op("vector", lambda e, W=W, SVX=SVX: e.reduce_max(
                   Bf[:, SVX:SVX + 1], PL[:, 0:W], axis=AX.X, negate=True),
               waits=wmx, ms=("mx", t))

        def emit_smx(t):
            h, jl = divmod(t, NQ)
            gp = t % 3
            W = (QB0 + jl + 1) * 128
            SVX = SVB0 + 16 * gp
            AUB = AUB0 + ANW * gp
            AN = ANB + ANW * gp
            wexp = [("mx", t)]
            if t >= 3:
                wexp += [("nrm", t - 3), ("rcp", t - 3)]
            op("scalar", lambda e, W=W, SVX=SVX, AUB=AUB: e.activation(
                   Bb[:, AUB:AUB + W], PL[:, 0:W], AF.Exp,
                   bias=Bf[:, SVX:SVX + 1], scale=1.0,
                   accum_out=Bf[:, SVX + 5:SVX + 6]),
               waits=wexp, ms=("exp", t))
            op("vector", lambda e, SVX=SVX: e.reciprocal(
                   Bf[:, SVX + 6:SVX + 7], Bf[:, SVX + 5:SVX + 6]),
               waits=[("exp", t)], ms=("rcp", t))
            wn = [("rcp", t)]
            if t >= 3:
                wn.append(("trl", t - 3))
            op("vector", lambda e, W=W, AUB=AUB, AN=AN: e.tensor_scalar_mul(
                   A[:, AN:AN + W], Bb[:, AUB:AUB + W], Bf[:, SVX + 6:SVX + 7]),
               waits=wn, ms=("nrm", t))

        def emit_trav(t):
            h, jl = divmod(t, NQ)
            klen = QB0 + jl + 1
            gp = t % 3
            AR, qb = qkvset(h)
            vb = qb + 2 * QW + 2 * KWA
            AN = ANB + ANW * gp
            AT = ANB + 3 * ANW + ANW * gp
            ngrp = (klen + 3) // 4
            for g in range(ngrp):
                kks = list(range(4 * g, min(4 * g + 4, klen)))
                tb = len(grp_hist) % 2
                wtr = [("nrm", t)]
                if len(grp_hist) >= 2:
                    wtr.append(grp_hist[-2])
                elif bankfree[4 + tb] is not None:
                    wtr.append(bankfree[4 + tb])
                for i, kk in enumerate(kks):
                    op("tensor",
                       lambda e, tb=tb, kk=kk, i=i, AN=AN: e.transpose(
                           trP[tb][:, i * 128:(i + 1) * 128],
                           A[:, AN + kk * 128:AN + (kk + 1) * 128], identb[:, :]),
                       waits=wtr if i == 0 else (),
                       ms=("trl", t) if kk == klen - 1 else ("trx", t, kk))
                wge = [("trl", t) if kks[-1] == klen - 1 else ("trx", t, kks[-1])]
                if t >= 3:
                    wge.append(("avmm", t - 3))
                gkey = ("grpev", t, g)
                op("vector",
                   lambda e, tb=tb, g=g, n=len(kks), AT=AT: e.tensor_copy(
                       A[:, AT + g * 512:AT + g * 512 + n * 128],
                       trP[tb][:, 0:n * 128]),
                   waits=wge, ms=gkey)
                grp_hist.append(gkey)
            for kk in range(klen):
                wav = [("grpev", t, kk // 4)]
                if kk == 0:
                    wav.append(("avev", t - 1) if t > 0 else p1done)
                    wav.append(("ldv", h))
                    if t == 0:
                        wav += bwait(6) + bwait(7)
                for dh in range(2):
                    op("tensor",
                       lambda e, dh=dh, kk=kk, klen=klen, vb=vb, AT=AT, AR=AR: e.matmul(
                           avP[dh][:, 0:128],
                           AR[:, vb + kk * 256 + dh * 128:vb + kk * 256 + (dh + 1) * 128],
                           A[:, AT + kk * 128:AT + (kk + 1) * 128],
                           start=(kk == 0), stop=(kk == klen - 1)),
                       waits=wav if (dh == 0 or kk == 0) else (),
                       ms=("avmm", t) if (kk == klen - 1 and dh == 1) else None)

        def emit_avev(t):
            h, jl = divmod(t, NQ)
            for dh in range(2):
                op("scalar",
                   lambda e, dh=dh, h=h, jl=jl: e.copy(
                       A[:, (2 * h + dh) * QW + jl * 128:(2 * h + dh) * QW + (jl + 1) * 128],
                       avP[dh][:, 0:128]),
                   waits=[("avmm", t)], ms=("avev", t) if dh == 1 else None)

        emit_head_loads(0)
        emit_head_loads(1)
        emit_head_loads(2)
        for u in range(T2 + 2):
            if u < T2:
                h, jl = divmod(u, NQ)
                if jl == 3 and 3 <= h + 2 < NH:
                    emit_head_loads(h + 2)
                emit_lg(u)
                emit_mx(u)
            if u - 2 >= 0:
                emit_trav(u - 2)
            if u < T2:
                emit_smx(u)
            if u - 2 >= 0:
                emit_avev(u - 2)

        p2done = ("avev", T2 - 1)
        bankfree = [None] * 8
        E.set_fence([("avmm", T2 - 1), p2done, grp_hist[-1],
                     ("ropeq", NH - 1), ("ropek", NH - 1)]
                    + dma_tail("sync") + dma_tail("gpsimd"))

        WOSTb4 = 0
        WOBFb4 = WOSTb4 + 2 * 1024 * 4
        ZSTb4 = WOBFb4 + 16384 * 2
        XRSb4 = ZSTb4 + 4 * 512 * 4
        assert XRSb4 + 4 * 512 * 4 <= 64512
        WOST4, WOBF4 = WOSTb4 // 4, WOBFb4 // 2
        ZST4, XRS4 = ZSTb4 // 4, XRSb4 // 4
        groupsM = [list(range(0, min(NM, 8)))] + ([list(range(8, NM))] if NM > 8 else [])
        NG = len(groupsM)
        zlast = {}
        xlast = {}

        def wo_half_free(nn, half):
            if nn == 0:
                return None
            return ("wommH0", nn - 1, NG - 1) if half == 0 else ("womm", nn - 1, NG - 1)

        for nn in range(8):
            for k2 in range(16):
                b = k2 % 2
                half = k2 // 8
                if k2 >= 2:
                    w = [("wocast", nn, k2 - 2)]
                elif nn > 0:
                    w = [("wocast", nn - 1, 14 + k2)]
                else:
                    w = [p2done]
                dma("sync", Bf[:, WOST4 + b * 1024:WOST4 + (b + 1) * 1024].rearrange("p (c m) -> p c m", m=512),
                    wo_d[k2 * 256:(k2 + 1) * 256, nn * 512:(nn + 1) * 512].rearrange("(c p) m -> p c m", p=128),
                    waits=w, ms=("wod", nn, k2))
                wc = [("wod", nn, k2)]
                whf = wo_half_free(nn, half)
                if whf is not None:
                    wc.append(whf)
                if nn == 0 and k2 < 2:
                    wc.append(p2done)
                op("scalar",
                   lambda e, b=b, k2=k2: e.copy(Bb[:, WOBF4 + k2 * 1024:WOBF4 + (k2 + 1) * 1024],
                                                Bf[:, WOST4 + b * 1024:WOST4 + (b + 1) * 1024]),
                   waits=wc, ms=("wocast", nn, k2))
            for gi, ms_ in enumerate(groupsM):
                for kk in range(KD):
                    for ii, m in enumerate(ms_):
                        wmm = [("wocast", nn, kk // 2)]
                        if kk == 0:
                            wmm += bwait(ii) + [p2done]
                        if kk == KD - 1 and ii == len(ms_) - 1:
                            mskey = ("womm", nn, gi)
                        elif kk == 15 and ii == len(ms_) - 1:
                            mskey = ("wommH0", nn, gi)
                        else:
                            mskey = None
                        op("tensor",
                           lambda e, m=m, ii=ii, kk=kk: e.matmul(
                               bank(ii),
                               A[:, kk * QW + m * 128:kk * QW + (m + 1) * 128],
                               Bb[:, WOBF4 + kk * 512:WOBF4 + (kk + 1) * 512],
                               start=(kk == 0), stop=(kk == KD - 1)),
                           waits=wmm if (ii == 0 or kk == 0) else (),
                           ms=mskey)
                for ii, m in enumerate(ms_):
                    zb = m % 4
                    wx = [p2done]
                    if ("x", zb) in xlast:
                        wx.append(xlast[("x", zb)])
                    dma("sync", Bf[:, XRS4 + zb * 512:XRS4 + (zb + 1) * 512],
                        xres_d[m * 128:(m + 1) * 128, nn * 512:(nn + 1) * 512],
                        waits=wx, ms=("xrsd", nn, m))
                    w = [("womm", nn, gi), ("xrsd", nn, m)]
                    if ("z", zb) in zlast:
                        w.append(zlast[("z", zb)])
                    mskey = ("zev", nn, m)
                    op("vector",
                       lambda e, ii=ii, zb=zb: e.tensor_add(
                           Bf[:, ZST4 + zb * 512:ZST4 + (zb + 1) * 512], bank(ii),
                           Bf[:, XRS4 + zb * 512:XRS4 + (zb + 1) * 512]),
                       waits=w, ms=mskey)
                    bankfree[ii] = mskey
                    xlast[("x", zb)] = mskey
                    op("vector",
                       lambda e, m=m, nn=nn, zb=zb: e.bn_stats(
                           stats[:, ST_BN + (m * 8 + nn) * 6:ST_BN + (m * 8 + nn + 1) * 6],
                           Bf[:, ZST4 + zb * 512:ZST4 + (zb + 1) * 512]),
                       waits=[("zev", nn, m)], ms=("bnst", m, nn))
                    dma("gpsimd", z_hbm[m][:, nn * 512:(nn + 1) * 512],
                        Bf[:, ZST4 + zb * 512:ZST4 + (zb + 1) * 512],
                        waits=[("zev", nn, m), ("bnst", m, nn)], ms=("zdma", nn, m))
                    zlast[("z", zb)] = ("zdma", nn, m)

        zall_list = [("zdma", 7, i) for i in range(NM)]
        E.set_fence([("womm", 7, NG - 1), ("bnst", groupsM[-1][-1], 7),
                     ("wocast", 7, 15)] + dma_tail("sync") + dma_tail("gpsimd"))

        TSLb5 = 0
        X2Fb5 = TSLb5 + 4 * 512 * 4
        X2Bb5 = X2Fb5 + 2 * 512 * 4
        TSL5, X2F5 = TSLb5 // 4, X2Fb5 // 4
        X2B5 = X2Bb5 // 2
        for m in range(NM):
            op("vector",
               lambda e, m=m: e.bn_aggr(stats[:, ST_MV + m * 2:ST_MV + (m + 1) * 2],
                                        stats[:, ST_BN + m * 48:ST_BN + (m + 1) * 48]),
               waits=[("bnst", m, n2) for n2 in range(8)], ms=("mv", m))
            op("scalar",
               lambda e, m=m: e.activation(stats[:, ST_RSTD + m:ST_RSTD + m + 1],
                                           stats[:, ST_MV + m * 2 + 1:ST_MV + m * 2 + 2],
                                           AF.Ln, bias=stats[:, ST_EPS:ST_EPS + 1], scale=1.0),
               waits=[("mv", m), ("epsv",)], ms=("lnv", m))
            op("scalar",
               lambda e, m=m: e.activation(stats[:, ST_RSTD + m:ST_RSTD + m + 1],
                                           stats[:, ST_RSTD + m:ST_RSTD + m + 1],
                                           AF.Exp, bias=0.0, scale=-0.5),
               waits=[("lnv", m)], ms=("rstd", m))
            op("vector",
               lambda e, m=m: e.tensor_mul(stats[:, ST_NMR + m:ST_NMR + m + 1],
                                           stats[:, ST_MV + m * 2:ST_MV + m * 2 + 1],
                                           stats[:, ST_RSTD + m:ST_RSTD + m + 1]),
               waits=[("rstd", m)], ms=("nmr", m))
            for nn in range(8):
                bx = nn % 4
                w = list(zall_list) + [("zdma", 7, m)]
                if nn >= 4:
                    w.append(("x2n", m, nn - 4))
                elif m > 0:
                    w.append(("x2n", m - 1, nn + 4))
                dma("sync", Bf[:, TSL5 + bx * 512:TSL5 + (bx + 1) * 512],
                    z_hbm[m][:, nn * 512:(nn + 1) * 512], waits=w, ms=("zrd", m, nn))
                tb = nn % 2
                wn = [("zrd", m, nn), ("nmr", m)]
                if nn >= 2:
                    wn.append(("x2dma", m, nn - 2))
                elif m > 0:
                    wn.append(("x2dma", m - 1, nn + 6))
                op("vector",
                   lambda e, m=m, bx=bx, tb=tb: e.tensor_scalar(
                       Bf[:, X2F5 + tb * 512:X2F5 + (tb + 1) * 512],
                       Bf[:, TSL5 + bx * 512:TSL5 + (bx + 1) * 512],
                       stats[:, ST_RSTD + m:ST_RSTD + m + 1],
                       stats[:, ST_NMR + m:ST_NMR + m + 1],
                       ALU.mult, ALU.subtract),
                   waits=wn, ms=("x2n", m, nn))
                dma("gpsimd", x2_hbm[m][:, nn * 512:(nn + 1) * 512],
                    Bf[:, X2F5 + tb * 512:X2F5 + (tb + 1) * 512],
                    waits=[("x2n", m, nn)], ms=("x2dma", m, nn))
                wxc = [("x2n", m, nn)]
                if nn >= 2:
                    wxc.append(("x2t", m, (nn - 2) * 4 + 3))
                elif m > 0:
                    wxc.append(("x2t", m - 1, 6 * 4 + 3 + (nn) * 4))
                op("scalar",
                   lambda e, tb=tb: e.copy(
                       Bb[:, X2B5 + tb * 512:X2B5 + (tb + 1) * 512],
                       Bf[:, X2F5 + tb * 512:X2F5 + (tb + 1) * 512]),
                   waits=wxc, ms=("x2c", m, nn))
                wrsp = [("x2c", m, nn)]
                if m > 0 and nn == 0:
                    wrsp.append(("rs", m - 1))
                op("vector",
                   lambda e, nn=nn, tb=tb: e.reduce_sum(
                       stats[:, ST_RSP + nn:ST_RSP + nn + 1],
                       Bb[:, X2B5 + tb * 512:X2B5 + (tb + 1) * 512], axis=AX.X),
                   waits=wrsp, ms=("rsp", m, nn))
                pbk = len(grp_hist) % 2
                wtr = [("x2c", m, nn), p2done]
                if len(grp_hist) >= 2:
                    wtr.append(grp_hist[-2])
                for t4 in range(4):
                    kkc = nn * 4 + t4
                    op("tensor",
                       lambda e, pbk=pbk, tb=tb, t4=t4: e.transpose(
                           trP[pbk][:, t4 * 128:(t4 + 1) * 128],
                           Bb[:, X2B5 + tb * 512 + t4 * 128:X2B5 + tb * 512 + (t4 + 1) * 128],
                           identb[:, :]),
                       waits=wtr if t4 == 0 else (), ms=("x2t", m, kkc))
                gkey = ("x2g", m, nn)
                op("vector",
                   lambda e, pbk=pbk, m=m, nn=nn: e.tensor_copy(
                       A[:, 0:32 * NM * 128].rearrange("p (c t) -> p c t", t=NM * 128)
                       [:, nn * 4:nn * 4 + 4, m * 128:(m + 1) * 128],
                       trP[pbk][:, 0:512].rearrange("p (c t) -> p c t", t=128)),
                   waits=[("x2t", m, nn * 4 + 3)], ms=gkey)
                grp_hist.append(gkey)
            op("vector",
               lambda e, m=m: e.reduce_sum(stats[:, ST_RS + m:ST_RS + m + 1],
                                           stats[:, ST_RSP:ST_RSP + 8], axis=AX.X),
               waits=[("rsp", m, n2) for n2 in range(8)], ms=("rs", m))
        op("vector",
           lambda e: e.tensor_scalar_mul(stats[:, ST_NB:ST_NB + NM], stats[:, ST_RS:ST_RS + NM],
                                         stats[:, ST_NEGFS:ST_NEGFS + 1]),
           waits=[("rs", m2) for m2 in range(NM)] + [("negfs",)], ms=("nb",))

        x2tall = ("x2g", NM - 1, 7)
        bankfree = [None] * 8
        E.set_fence([("x2t", NM - 1, 31), ("nb",), ("x2c", NM - 1, 7)]
                    + dma_tail("sync") + dma_tail("gpsimd"))

        FWSTb6 = 0
        FWBFb6 = FWSTb6 + 2 * 1024
        X2RSb6 = FWBFb6 + 16384 * 2
        Z2STb6 = X2RSb6 + 4 * 512 * 4
        assert Z2STb6 + 4 * 512 * 4 <= 64512
        FWST6 = FWSTb6
        FWBF6 = FWBFb6 // 2
        X2RS6, Z2ST6 = X2RSb6 // 4, Z2STb6 // 4
        zlast = {}
        xlast = {}

        for nn in range(8):
            for k2 in range(16):
                b = k2 % 2
                half = k2 // 8
                if k2 >= 2:
                    w = [("fwcast", nn, k2 - 2)]
                elif nn > 0:
                    w = [("fwcast", nn - 1, 14 + k2)]
                else:
                    w = [("nb",), x2tall]
                dma("sync", Bu[:, FWST6 + b * 1024:FWST6 + (b + 1) * 1024].rearrange("p (c m) -> p c m", m=512),
                    fw_d[k2 * 256:(k2 + 1) * 256, nn * 512:(nn + 1) * 512].rearrange("(c p) m -> p c m", p=128),
                    waits=w, ms=("fwd", nn, k2))
                wc = [("fwd", nn, k2)]
                if nn > 0:
                    wc.append(("fmmH0", nn - 1, NG - 1) if half == 0
                              else ("fmm", nn - 1, NG - 1))
                if nn == 0 and k2 < 2:
                    wc += [("nb",), x2tall]
                op("scalar",
                   lambda e, b=b, k2=k2: e.copy(Bb[:, FWBF6 + k2 * 1024:FWBF6 + (k2 + 1) * 1024],
                                                Bu[:, FWST6 + b * 1024:FWST6 + (b + 1) * 1024]),
                   waits=wc, ms=("fwcast", nn, k2))
            for gi, ms_ in enumerate(groupsM):
                for kk in range(KD):
                    for ii, m in enumerate(ms_):
                        wmm = [("fwcast", nn, kk // 2)]
                        if kk == 0:
                            wmm += bwait(ii) + [x2tall]
                        if kk == KD - 1 and ii == len(ms_) - 1:
                            mskey = ("fmm", nn, gi)
                        elif kk == 15 and ii == len(ms_) - 1:
                            mskey = ("fmmH0", nn, gi)
                        else:
                            mskey = None
                        op("tensor",
                           lambda e, m=m, ii=ii, kk=kk: e.matmul(
                               bank(ii),
                               A[:, kk * NM * 128 + m * 128:kk * NM * 128 + (m + 1) * 128],
                               Bb[:, FWBF6 + kk * 512:FWBF6 + (kk + 1) * 512],
                               start=(kk == 0), stop=(kk == KD - 1)),
                           waits=wmm if (ii == 0 or kk == 0) else (),
                           ms=mskey)
                for ii, m in enumerate(ms_):
                    zb = m % 4
                    wx = [("nb",), x2tall, ("x2dma", m, nn)]
                    if ("x", zb) in xlast:
                        wx.append(xlast[("x", zb)])
                    dma("sync", Bf[:, X2RS6 + zb * 512:X2RS6 + (zb + 1) * 512],
                        x2_hbm[m][:, nn * 512:(nn + 1) * 512],
                        waits=wx, ms=("x2sl", nn, m))
                    op("vector",
                       lambda e, m=m, zb=zb: e.tensor_scalar_add(
                           Bf[:, X2RS6 + zb * 512:X2RS6 + (zb + 1) * 512],
                           Bf[:, X2RS6 + zb * 512:X2RS6 + (zb + 1) * 512],
                           stats[:, ST_NB + m:ST_NB + m + 1]),
                       waits=[("x2sl", nn, m), ("nb",)], ms=("xadj", nn, m))
                    w = [("fmm", nn, gi), ("xadj", nn, m), ("x2sl", nn, m), ("fs2",)]
                    if ("z", zb) in zlast:
                        w.append(zlast[("z", zb)])
                    mskey = ("z2ev", nn, m)
                    op("vector",
                       lambda e, ii=ii, zb=zb: e.scalar_tensor_tensor(
                           Bf[:, Z2ST6 + zb * 512:Z2ST6 + (zb + 1) * 512], bank(ii),
                           stats[:, ST_FS2:ST_FS2 + 1],
                           Bf[:, X2RS6 + zb * 512:X2RS6 + (zb + 1) * 512],
                           ALU.mult, ALU.add),
                       waits=w, ms=mskey)
                    bankfree[ii] = mskey
                    xlast[("x", zb)] = mskey
                    wb = [("z2ev", nn, m)]
                    if nn == 0:
                        wb.append(("mv", NM - 1))
                    op("vector",
                       lambda e, m=m, nn=nn, zb=zb: e.bn_stats(
                           stats[:, ST_BN + (m * 8 + nn) * 6:ST_BN + (m * 8 + nn + 1) * 6],
                           Bf[:, Z2ST6 + zb * 512:Z2ST6 + (zb + 1) * 512]),
                       waits=wb, ms=("bnst2", m, nn))
                    dma("gpsimd", z2_hbm[m][:, nn * 512:(nn + 1) * 512],
                        Bf[:, Z2ST6 + zb * 512:Z2ST6 + (zb + 1) * 512],
                        waits=[("bnst2", m, nn)], ms=("z2dma", nn, m))
                    zlast[("z", zb)] = ("z2dma", nn, m)

        z2all_list = [("z2dma", 7, i) for i in range(NM)]
        E.set_fence([("fmm", 7, NG - 1), ("bnst2", groupsM[-1][-1], 7),
                     ("fwcast", 7, 15)] + dma_tail("sync") + dma_tail("gpsimd"))

        T7b = 0
        O7b = T7b + 4 * 512 * 4
        T7, O7 = T7b // 4, O7b // 4
        for m in range(NM):
            op("vector",
               lambda e, m=m: e.bn_aggr(stats[:, ST_MV + m * 2:ST_MV + (m + 1) * 2],
                                        stats[:, ST_BN + m * 48:ST_BN + (m + 1) * 48]),
               waits=[("bnst2", m, n2) for n2 in range(8)] + [("rstd", NM - 1), ("nmr", NM - 1)],
               ms=("mv2", m))
            op("scalar",
               lambda e, m=m: e.activation(stats[:, ST_RSTD + m:ST_RSTD + m + 1],
                                           stats[:, ST_MV + m * 2 + 1:ST_MV + m * 2 + 2],
                                           AF.Ln, bias=stats[:, ST_EPS:ST_EPS + 1], scale=1.0),
               waits=[("mv2", m), ("epsv",)], ms=("lnv2", m))
            op("scalar",
               lambda e, m=m: e.activation(stats[:, ST_RSTD + m:ST_RSTD + m + 1],
                                           stats[:, ST_RSTD + m:ST_RSTD + m + 1],
                                           AF.Exp, bias=0.0, scale=-0.5),
               waits=[("lnv2", m)], ms=("rstd2", m))
            op("vector",
               lambda e, m=m: e.tensor_mul(stats[:, ST_NMR + m:ST_NMR + m + 1],
                                           stats[:, ST_MV + m * 2:ST_MV + m * 2 + 1],
                                           stats[:, ST_RSTD + m:ST_RSTD + m + 1]),
               waits=[("rstd2", m)], ms=("nmr2", m))
            for nn in range(8):
                bx = nn % 4
                w = list(z2all_list) + [("z2dma", 7, m)]
                if nn >= 4:
                    w.append(("t2n", m, nn - 4))
                elif m > 0:
                    w.append(("t2n", m - 1, nn + 4))
                dma("sync", Bf[:, T7 + bx * 512:T7 + (bx + 1) * 512],
                    z2_hbm[m][:, nn * 512:(nn + 1) * 512], waits=w, ms=("z2rd", m, nn))
                tb = nn % 2
                wn = [("z2rd", m, nn), ("nmr2", m)]
                if nn >= 2:
                    wn.append(("outdma", m, nn - 2))
                elif m > 0:
                    wn.append(("outdma", m - 1, nn + 6))
                op("vector",
                   lambda e, m=m, bx=bx, tb=tb: e.tensor_scalar(
                       Bf[:, O7 + tb * 512:O7 + (tb + 1) * 512],
                       Bf[:, T7 + bx * 512:T7 + (bx + 1) * 512],
                       stats[:, ST_RSTD + m:ST_RSTD + m + 1],
                       stats[:, ST_NMR + m:ST_NMR + m + 1],
                       ALU.mult, ALU.subtract),
                   waits=wn, ms=("t2n", m, nn))
                dma("gpsimd", out_d[m * 128:(m + 1) * 128, nn * 512:(nn + 1) * 512],
                    Bf[:, O7 + tb * 512:O7 + (tb + 1) * 512],
                    waits=[("t2n", m, nn)], ms=("outdma", m, nn))

        for i in range(QK["sync"]):
            E.final_wait("sync", f"s{i}")
        for i in range(QK["gpsimd"]):
            E.final_wait("gpsimd", f"g{i}")

    E0.begin_count()
    walk(0, E0)
    E1.begin_count()
    walk(1, E1)

    sems = {}
    with ExitStack() as semstack:
        for name in (["t", "v", "s", "p", "rr"] + [f"s{i}" for i in range(12)] + [f"g{i}" for i in range(8)]):
            sems[name] = semstack.enter_context(nc.semaphore(f"sem_{name}"))

        def section(engname):
            def body(eng):
                if engname == "sync":
                    eng.dma_start(out=rolesb[:, :], in_=role_d).then_inc(sems["rr"], 16)
                eng.wait_ge(sems["rr"], 16)
                with eng.register(f"role_{engname}") as r:
                    eng.reg_load(r, rolesb[0:1, 0:1])
                    for i in nc.Switch(engines=[eng], index=[eng.snap(r)], n=2):
                        EE = E0 if i == 0 else E1
                        EE.begin_emit(engname, eng, sems)
                        walk(i, EE)
            return body

        with nc.Block() as block:
            block.sync(section("sync"))
            block.gpsimd(section("gpsimd"))
            block.tensor(section("tensor"))
            block.vector(section("vector"))
            block.scalar(section("scalar"))
    es.close()
    return nc



def _rot_perm():
    p = np.arange(D)
    for h in range(NH):
        base = h * HD
        p[base:base + 64] = np.concatenate([np.arange(0, 64, 2), np.arange(1, 64, 2)]) + base
    return p


def _tables(idx, width):
    inv_freq = (1.0 / (10000.0 ** (np.arange(0, 64, 2, dtype=np.float32) / np.float32(64.0)))).astype(np.float32)
    t = idx.astype(np.float32)
    sinus = np.einsum("i,j->ij", t, inv_freq).astype(np.float32)
    sn = np.sin(sinus).astype(np.float32).T
    cs = np.cos(sinus).astype(np.float32).T
    out = np.zeros((128, width), np.float32)
    n = len(idx)
    out[0:32, 0:n] = cs
    out[32:64, 0:n] = cs
    out[64:96, 0:n] = sn
    out[96:128, 0:n] = sn
    return out.astype(BFNP)


def _in_maps(x, wq, wk, wv, wo, fw, fs):
    ident = np.eye(128, dtype=np.float32).astype(BFNP)
    mval = np.float32(NEGV / 16.0)
    tri = np.where(np.arange(128)[:, None] >= np.arange(128)[None, :],
                   np.float32(0.0), mval).astype(BFNP)
    NMXT = ACUT * 128
    maps = []
    for c in range(8):
        b, role = c // 2, c % 2
        if role == 0:
            myidx = np.arange(0, ACUT * 128)
        else:
            myidx = np.arange(ACUT * 128, S)
        xres = np.zeros((NMXT, D), np.float32)
        xres[0:len(myidx)] = x[b][myidx]
        maps.append({
            "xT": np.ascontiguousarray(x[b].T),
            "xres": xres,
            "wq": wq, "wk": wk, "wv": wv, "wo": wo, "fw": fw, "fs": fs,
            "rolein": np.array([[role]], dtype=np.uint32),
            "tabq": _tables(myidx, NMXT), "tabk": _tables(np.arange(S), S),
            "maska": tri, "ident": ident,
        })
    return maps


def kernel(L0_attn_hidden_states, L0_attn_Wq, L0_attn_Wk, L0_attn_Wv, L0_attn_Wo,
           L0_ffn_kernel, L0_ffn_scale):
    x = np.asarray(L0_attn_hidden_states)
    perm = _rot_perm()
    wq = np.ascontiguousarray(np.asarray(L0_attn_Wq)[:, perm])
    wk = np.ascontiguousarray(np.asarray(L0_attn_Wk)[:, perm])
    wv = np.ascontiguousarray(np.asarray(L0_attn_Wv))
    wo = np.ascontiguousarray(np.asarray(L0_attn_Wo))
    fw = np.ascontiguousarray(np.asarray(L0_ffn_kernel).view(np.uint8))
    fs = np.asarray(L0_ffn_scale, dtype=np.float32).reshape(1, 1)

    nc = build_nc()
    in_maps = _in_maps(x, wq, wk, wv, wo, fw, fs)
    res = run_bass_kernel_spmd(nc, in_maps, core_ids=list(range(8)))
    out = np.empty((4, S, D), np.float32)
    for c in range(8):
        b, role = c // 2, c % 2
        if role == 0:
            out[b, 0:ACUT * 128, :] = res.results[c]["out"][0:ACUT * 128]
        else:
            out[b, ACUT * 128:S, :] = res.results[c]["out"][0:S - ACUT * 128]
    return out
